# version 17
# baseline (speedup 1.0000x reference)
import numpy as np
import ml_dtypes
from contextlib import ExitStack

import concourse.bass as bass
import concourse.mybir as mybir
from concourse.bass_utils import run_bass_kernel_spmd

F32 = mybir.dt.float32
BF16 = mybir.dt.bfloat16
AF = mybir.ActivationFunctionType
ALU = mybir.AluOpType
AX = mybir.AxisListType.X

S = 4096
D = 1024
HID = 4096
NH = 16
EH = 64
NT = S // 128
NQ = S // 512
EPS = 1e-5

ENGS = ("sync", "gpsimd", "tensor", "vector", "scalar")
ALL_PHASES = ("p0", "qkv", "attn", "wo", "fa", "fb", "fc", "fd")


class Ch:
    __slots__ = ("h", "n", "name")

    def __init__(self, h, name=""):
        self.h = h
        self.n = 0
        self.name = name


class Prog:
    def __init__(self, paranoid=None):
        self.q = {e: [] for e in ENGS}
        self.paranoid = paranoid
        self.last_edge = {}

    def op(self, eng, fn, waits=(), inc=None, inck=1, edge=True):
        w = tuple((c.h, v) for (c, v) in waits if c is not None and v > 0)
        tgt = None
        if inc is not None:
            inc.n += inck
            tgt = (inc.h, inck)
        pwait = pinc = None
        if self.paranoid is not None and eng in self.paranoid:
            pwait = self.last_edge.get(eng)
            if not edge:
                pass
            elif tgt is not None:
                self.last_edge[eng] = (inc.h, inc.n)
            else:
                pc = self.paranoid[eng]
                pc.n += 1
                pinc = (pc.h, 1)
                self.last_edge[eng] = (pc.h, pc.n)

        def run(e, fn=fn, w=w, tgt=tgt, pwait=pwait, pinc=pinc):
            if pwait is not None:
                e.wait_ge(*pwait)
            for h, v in w:
                e.wait_ge(h, v)
            r = fn(e)
            if tgt is not None:
                r.then_inc(*tgt)
            if pinc is not None:
                r.then_inc(pinc[0], pinc[1], skip_validation=True)

        self.q[eng].append(run)
        return inc.n if inc is not None else 0

    def wait(self, eng, waits):
        w = tuple((c.h, v) for (c, v) in waits if c is not None and v > 0)
        if not w:
            return

        def run(e, w=w):
            for h, v in w:
                e.wait_ge(h, v)

        self.q[eng].append(run)


def _ap(x):
    return x if isinstance(x, bass.AP) else x[:, :]


def build_nc(upto="fd", taps=(), paranoid=False):
    nphase = ALL_PHASES.index(upto) + 1
    phases = set(ALL_PHASES[:nphase])
    nc = bass.Bass("TRN2", target_bir_lowering=False, debug=False)

    x_h = nc.dram_tensor("x", [S, D], F32, kind="ExternalInput").ap()
    wqkv_h = nc.dram_tensor("wqkv", [3, D, D], F32, kind="ExternalInput").ap()
    wo_h = nc.dram_tensor("wo", [D, D], F32, kind="ExternalInput").ap()
    w1t_h = nc.dram_tensor("w1t", [D, HID], F32, kind="ExternalInput").ap()
    w2t_h = nc.dram_tensor("w2t", [D, HID], F32, kind="ExternalInput").ap()
    h1t_h = nc.dram_tensor("h1t", [HID, HID], F32, kind="ExternalInput").ap()
    h2t_h = nc.dram_tensor("h2t", [HID, HID], F32, kind="ExternalInput").ap()
    w3t_h = nc.dram_tensor("w3t", [HID, D], F32, kind="ExternalInput").ap()
    lnwb_h = nc.dram_tensor("lnwb", [128, 16], F32, kind="ExternalInput").ap()
    scl_h = nc.dram_tensor("scl", [128, 128], F32, kind="ExternalInput").ap()
    s3b_h = nc.dram_tensor("s3b", [128, D], F32, kind="ExternalInput").ap()
    ident_h = nc.dram_tensor("ident", [128, 128], BF16, kind="ExternalInput").ap()
    nident_h = nc.dram_tensor("nident", [128, 128], BF16, kind="ExternalInput").ap()
    bmask_h = nc.dram_tensor("bmask", [128, 896], BF16, kind="ExternalInput").ap()
    konst_h = nc.dram_tensor("konst", [128, 8], F32, kind="ExternalInput").ap()

    out_h = nc.dram_tensor("out", [S, D], F32, kind="ExternalOutput").ap()

    def scratch(name, shape, dtype):
        kind = "ExternalOutput" if name in taps else "Internal"
        return nc.dram_tensor(name, shape, dtype, kind=kind).ap()

    xpu_h = scratch("t_xpu", [S, D], F32)
    x1_h = scratch("t_x1", [S, D], F32)
    qT_h = scratch("t_qT", [NH * EH, S], BF16)
    kT_h = scratch("t_kT", [NH * EH, S], BF16)
    v_h = scratch("t_v", [S, D], BF16)
    h0_h = scratch("t_h0", [HID, S], BF16)
    ha_h = scratch("t_ha", [HID, S], BF16)
    hb_h = scratch("t_hb", [HID, S], BF16)
    yT_tap = scratch("t_yT", [128, 8 * S], BF16) if "t_yT" in taps else None
    u2T_tap = scratch("t_u2T", [128, 8 * S], BF16) if "t_u2T" in taps else None
    oall_tap = scratch("t_oall", [128, NT * D], BF16) if "t_oall" in taps else None

    st = ExitStack()
    final_waits = []

    with st:
        def sem(name):
            return Ch(st.enter_context(nc.semaphore(name)), name)

        def sem2(name):
            return [sem(name + "a"), sem(name + "b")]

        def sb(stk, name, shape, dtype):
            return stk.enter_context(nc.sbuf_tensor(name, shape, dtype))

        if paranoid:
            P_par = {e: sem("par_" + e) for e in ("scalar", "vector", "gpsimd")}
        else:
            P_par = None
        P = Prog(P_par)
        s_tap = sem("s_tap")

        ps_quad = st.enter_context(nc.psum_tensor("ps_quad", [128, 2048], F32))
        ps_a = ps_quad[:, 0:512]
        ps_b = ps_quad[:, 512:1024]
        ps_st = ps_quad[:, 1024:2048]
        ps_o = st.enter_context(nc.psum_tensor("ps_o", [128, 512], F32))
        ps_m = st.enter_context(nc.psum_tensor("ps_m", [128, 512], F32))
        pt_a = st.enter_context(nc.psum_tensor("pt_a", [128, 1024], BF16))
        pt_b = st.enter_context(nc.psum_tensor("pt_b", [128, 1024], BF16))
        PS4 = [ps_quad[:, i * 512:(i + 1) * 512] for i in range(4)]

        c_ident = sb(st, "c_ident", [128, 128], BF16)
        c_nident = sb(st, "c_nident", [128, 128], BF16)
        c_bmask = sb(st, "c_bmask", [128, 896], BF16)
        c_lnwb = sb(st, "c_lnwb", [128, 16], F32)
        c_scl = sb(st, "c_scl", [128, 128], F32)
        c_ones = sb(st, "c_ones", [1, 8], F32)
        c_konst = sb(st, "c_konst", [128, 8], F32)
        KONSTS = (EPS, -0.5, 0.125, -1.0, 1.0, 1.0 / D)
        for ki, kv in enumerate(KONSTS):
            nc.const_aps.register(F32, kv, c_konst[:, ki:ki + 1])

        s_const = sem("s_const")
        for dst, src in ((c_ident, ident_h), (c_nident, nident_h), (c_bmask, bmask_h),
                         (c_lnwb, lnwb_h), (c_scl, scl_h),
                         (c_konst, konst_h)):
            P.op("sync", lambda e, d=dst, s=src: e.dma_start(out=d[:, :], in_=s[:, :]),
                 inc=s_const, inck=16)
        P.op("vector", lambda e: e.memset(c_ones[:, :], 1.0), inc=s_const)
        CONST = s_const.n

        def barrier(name, drains=()):
            c = sem(name)
            P.wait("sync", list(drains))
            for e in ENGS:
                P.op(e, lambda eng: eng.nop(), inc=c, edge=False)
            for e in ENGS:
                P.wait(e, [(c, len(ENGS))])

        def emit_ln_tile_g(ti, bufs, src_sb, yT_dst, affine, sems, extra_u_cb=None,
                           ld_wait=None):
            b = ti % 2
            s_v, s_s, s_t = sems["v"], sems["s"], sems["t"]
            ptb = pt_a if (ti % 2 == 0) else pt_b
            P.op("vector", lambda e, b=b: e.reduce_sum(bufs["mv"][b][:, 0:1], src_sb[:, :],
                                                       axis=AX),
                 waits=([ld_wait] if ld_wait else []))
            P.op("vector", lambda e, b=b: e.scalar_tensor_tensor(
                bufs["sq"][b][:, :], src_sb[:, :], 1.0, src_sb[:, :],
                ALU.bypass, ALU.mult, accum_out=bufs["mv"][b][:, 1:2]), inc=s_v)
            mv_ready = s_v.n
            P.wait("scalar", [(s_v, mv_ready)])
            P.op("scalar", lambda e, b=b: e.mul(bufs["mn"][b][:, :],
                                                bufs["mv"][b][:, 0:1], 1.0 / D))
            P.op("scalar", lambda e, b=b: e.activation(out=bufs["msq"][b][:, :],
                                                       in_=bufs["mn"][b][:, :],
                                                       func=AF.Square))
            P.op("scalar", lambda e, b=b: e.mul(bufs["var"][b][:, :],
                                                bufs["msq"][b][:, :], -1.0))
            P.op("scalar", lambda e, b=b: e.add(bufs["var"][b][:, :],
                                                bufs["var"][b][:, :], EPS))
            P.op("scalar", lambda e, b=b: e.activation(out=bufs["tl"][b][:, :],
                                                       in_=bufs["mv"][b][:, 1:2],
                                                       func=AF.Ln, scale=1.0 / D,
                                                       bias=bufs["var"][b][:, :]))
            P.op("scalar", lambda e, b=b: e.activation(out=bufs["rs"][b][:, :],
                                                       in_=bufs["tl"][b][:, :],
                                                       func=AF.Exp, scale=-0.5), inc=s_s)
            rstd_ready = s_s.n
            yield
            P.wait("vector", [(s_s, rstd_ready)])
            P.op("vector", lambda e, b=b: e.scalar_tensor_tensor(
                bufs["nm"][b][:, :], bufs["mn"][b][:, :], -1.0, bufs["rs"][b][:, :],
                ALU.mult, ALU.mult), inc=s_v)
            P.wait("vector", [(s_v, s_v.n)])
            P.op("vector", lambda e, b=b: e.tensor_scalar(
                bufs["u"][b][:, :], src_sb[:, :], bufs["rs"][b][:, 0:1],
                bufs["nm"][b][:, 0:1], ALU.mult, ALU.add), inc=s_v)
            u_ready = s_v.n
            if extra_u_cb is not None:
                extra_u_cb(bufs["u"][b])
            P.op("scalar", lambda e, b=b: e.copy(out=bufs["ubf"][b][:, :],
                                                 in_=bufs["u"][b][:, :]),
                 waits=[(s_t, sems["tp_done"].get(ti - 2, 0)), (s_v, u_ready)], inc=s_s)
            ubf_ready = (s_s, s_s.n)
            trd = sems["tr_done"].get(ti - 2)
            P.wait("tensor", [ubf_ready,
                              (trd[0] if trd else None, trd[1] if trd else 0),
                              (s_const, CONST)])
            for f in range(8):
                P.op("tensor", lambda e, b=b, f=f, ptb=ptb: e.transpose(
                    ptb[:, f * 128:(f + 1) * 128], bufs["ubf"][b][:, f * 128:(f + 1) * 128],
                    c_ident[:, :]), inc=(s_t if f == 7 else None))
            sems["tp_done"][ti] = s_t.n
            tr_ready = s_t.n
            yield
            if affine:
                P.wait("vector", [(s_t, tr_ready)])
                for f in range(8):
                    P.op("vector", lambda e, f=f, ti=ti, ptb=ptb: e.tensor_scalar(
                        yT_dst[:, f * S + ti * 128:f * S + ti * 128 + 128],
                        ptb[:, f * 128:(f + 1) * 128],
                        c_lnwb[:, f:f + 1], c_lnwb[:, 8 + f:9 + f], ALU.mult, ALU.add),
                        inc=(s_v if f == 7 else None))
                sems["tr_done"][ti] = (s_v, s_v.n)
            else:
                P.wait("scalar", [(s_t, tr_ready)])
                ydst = yT_dst[:, 0:8 * S].rearrange("p (f t) -> p f t", t=S)
                P.op("scalar", lambda e, ti=ti, ptb=ptb, ydst=ydst: e.copy(
                    out=ydst[:, :, ti * 128:ti * 128 + 128],
                    in_=ptb[:, :].rearrange("p (f t) -> p f t", t=128)),
                    inc=s_s)
                sems["tr_done"][ti] = (s_s, s_s.n)

        def emit_ln_tile(*a, **kw):
            for _ in emit_ln_tile_g(*a, **kw):
                pass

        def make_ln_bufs(stk, tag):
            return {
                "sq": [sb(stk, tag + "sq", [128, D], F32)] * 2,
                "mn": [sb(stk, tag + "mn%d" % i, [128, 1], F32) for i in range(2)],
                "msq": [sb(stk, tag + "ms%d" % i, [128, 1], F32) for i in range(2)],
                "var": [sb(stk, tag + "va%d" % i, [128, 1], F32) for i in range(2)],
                "mv": [sb(stk, tag + "mv%d" % i, [128, 2], F32) for i in range(2)],
                "rs": [sb(stk, tag + "rs%d" % i, [128, 1], F32) for i in range(2)],
                "nm": [sb(stk, tag + "nm%d" % i, [128, 1], F32) for i in range(2)],
                "tl": [sb(stk, tag + "tl%d" % i, [128, 1], F32) for i in range(2)],
                "u": [sb(stk, tag + "u%d" % i, [128, D], F32) for i in range(2)],
                "ubf": [sb(stk, tag + "ubf%d" % i, [128, D], BF16) for i in range(2)],
            }

        stA = ExitStack()
        arenaA = stA.enter_context(nc.sbuf_tensor("arenaA", [128, 8 * S], BF16))

        s0_ld = sem2("s0_ld")
        s0_v = sem("s0_v")
        s0_s = sem("s0_s")
        s0_t = sem("s0_t")
        s0_st = sem2("s0_st")
        XPU_ST_ALL = []

        p0_chunk = {}
        p0_tile = {}
        if "p0" in phases:
            st0 = ExitStack()
            ln0 = make_ln_bufs(st0, "p0")
            xin0 = [sb(st0, "p0xin%d" % i, [128, D], F32) for i in range(2)]
            xpu0 = [sb(st0, "p0xpu%d" % i, [128, D], F32) for i in range(2)]
            sems0 = {"v": s0_v, "s": s0_s, "t": s0_t, "tr_done": {}, "tp_done": {}}
            xin_free = {}
            st_done = {}

        sq_w = sem2("sq_w")
        sq_wv = sem("sq_wv")
        sq_t = sem("sq_t")
        sq_s = sem("sq_s")
        sq_s2 = sem("sq_s2")
        sq_st = sem2("sq_st")
        sq_st2 = sem2("sq_st2")
        QKV_ST_ALL = []
        if "qkv" in phases:
            stq = ExitStack()
            wqk_sb = sb(stq, "qkw", [128, 16384], BF16)
            wvbuf = sb(stq, "qkwv", [128, 8 * 1024], BF16)
            ev = [sb(stq, "qkev%d" % i, [128, 512], BF16) for i in range(2)]
            vst = [sb(stq, "qkvst%d" % i, [128, 1024], BF16) for i in range(2)]
            wq3 = wqkv_h.rearrange("m (dt p) c -> m p dt c", p=128)
            for m in range(2):
                for ot in range(8):
                    idx = m * 8 + ot
                    P.op("gpsimd", lambda e, m=m, ot=ot, idx=idx: e.dma_start(
                        out=wqk_sb[:, idx * 1024:(idx + 1) * 1024].rearrange(
                            "p (dt c) -> p dt c", c=128),
                        in_=wq3[m, :, :, ot * 128:(ot + 1) * 128]),
                        inc=sq_w[m], inck=16)
            P.op("gpsimd", lambda e: e.dma_start(
                out=wvbuf[:, :].rearrange("p (dt c) -> p dt c", c=1024),
                in_=wq3[2, :, :, :]), inc=sq_wv, inck=16)
            wv_ready = sq_wv.n
            qkv_state = {"kpsum": 0, "kev": 0}
            ps_free = {}
            st_ev = {}
            vst_done = {}

            def emit_qk_unit(c, idx):
                m = idx // 8
                ks = qkv_state["kpsum"]
                qkv_state["kpsum"] += 1
                ps = PS4[ks % 4]
                pf = ps_free.get(ks - 4)
                P.wait("tensor", [(sq_w[m], 128), p0_chunk[c]] + ([pf] if pf else []))
                for dt in range(8):
                    P.op("tensor", lambda e, idx=idx, dt=dt, c=c, ps=ps: e.matmul(
                        _ap(ps),
                        wqk_sb[:, idx * 1024 + dt * 128:idx * 1024 + (dt + 1) * 128],
                        arenaA[:, dt * S + c * 512:dt * S + c * 512 + 512],
                        start=(dt == 0), stop=(dt == 7)),
                        inc=(sq_t if dt == 7 else None))
                kev = qkv_state["kev"]
                qkv_state["kev"] += 1
                b2 = kev % 2
                sev = st_ev.get(kev - 2)
                cch = sq_s if b2 == 0 else sq_s2
                if b2 == 0:
                    P.op("scalar", lambda e, b2=b2, ps=ps: e.copy(
                        out=ev[b2][:, :], in_=_ap(ps)),
                        waits=[(sq_t, sq_t.n)] + ([sev] if sev else []), inc=cch)
                else:
                    P.op("vector", lambda e, b2=b2, ps=ps: e.tensor_copy(
                        ev[b2][:, :], _ap(ps)),
                        waits=[(sq_t, sq_t.n)] + ([sev] if sev else []), inc=cch)
                ps_free[ks] = (cch, cch.n)
                dst = qT_h if m == 0 else kT_h
                ot = idx % 8
                P.op("sync", lambda e, dst=dst, ot=ot, c=c, b2=b2: e.dma_start(
                    out=dst[ot * 128:(ot + 1) * 128, c * 512:(c + 1) * 512],
                    in_=ev[b2][:, :]),
                    waits=[(cch, cch.n)], inc=sq_st[b2], inck=16)
                st_ev[kev] = (sq_st[b2], sq_st[b2].n)

            def emit_v_unit(tt, eh):
                b = tt % 2
                ks = qkv_state["kpsum"]
                qkv_state["kpsum"] += 1
                ps = PS4[ks % 4]
                pf = ps_free.get(ks - 4)
                P.wait("tensor", [(sq_wv, wv_ready), p0_tile[tt]] +
                       ([pf] if pf else []))
                for dt in range(8):
                    P.op("tensor", lambda e, tt=tt, dt=dt, eh=eh, ps=ps: e.matmul(
                        _ap(ps), arenaA[:, dt * S + tt * 128:dt * S + tt * 128 + 128],
                        wvbuf[:, dt * 1024 + eh * 512:dt * 1024 + eh * 512 + 512],
                        start=(dt == 0), stop=(dt == 7)),
                        inc=(sq_t if dt == 7 else None))
                cch = sq_s if eh == 0 else sq_s2
                vstp = vst_done.get(tt - 2)
                if eh == 0:
                    P.op("scalar", lambda e, b=b, eh=eh, ps=ps: e.copy(
                        out=vst[b][:, eh * 512:(eh + 1) * 512], in_=_ap(ps)),
                        waits=[(sq_t, sq_t.n)] + ([vstp] if vstp else []), inc=cch)
                else:
                    P.op("vector", lambda e, b=b, eh=eh, ps=ps: e.tensor_copy(
                        vst[b][:, eh * 512:(eh + 1) * 512], _ap(ps)),
                        waits=[(sq_t, sq_t.n)] + ([vstp] if vstp else []), inc=cch)
                ps_free[ks] = (cch, cch.n)
                if eh == 1:
                    P.op("sync", lambda e, b=b, tt=tt: e.dma_start(
                        out=v_h[tt * 128:(tt + 1) * 128, :], in_=vst[b][:, :]),
                        waits=[(sq_s, sq_s.n), (sq_s2, sq_s2.n)], inc=sq_st2[b], inck=16)
                    vst_done[tt] = (sq_st2[b], sq_st2[b].n)

            def qkv_unit_list():
                out = []
                for c in range(8):
                    for idx in range(16):
                        out.append(("qk", c, idx))
                    for tt in range(4 * c, 4 * c + 4):
                        for eh in range(2):
                            out.append(("v", tt, eh))
                return out
            units = qkv_unit_list()
            uidx = [0]

            def qkv_pump():
                while uidx[0] < len(units):
                    kind, a1, a2 = units[uidx[0]]
                    if kind == "qk" and a1 not in p0_chunk:
                        return
                    if kind == "v" and a1 not in p0_tile:
                        return
                    (emit_qk_unit if kind == "qk" else emit_v_unit)(a1, a2)
                    uidx[0] += 1
        else:
            def qkv_pump():
                pass

        if "p0" in phases:
            def xpu_cb_f(ti, b):
                def xpu_cb(u_f32):
                    std_prev = st_done.get(ti - 2)
                    P.op("vector", lambda e, b=b, u_f32=u_f32: e.tensor_add(
                        xpu0[b][:, :], xin0[b][:, :], u_f32[:, :]),
                        waits=[std_prev] if std_prev else [], inc=s0_v)
                    xin_free[ti] = s0_v.n
                    P.op("scalar", lambda e, ti=ti, b=b: e.dma_start(
                        out=xpu_h[ti * 128:(ti + 1) * 128, :], in_=xpu0[b][:, :]),
                        waits=[(s0_v, xin_free[ti])], inc=s0_st[b], inck=16)
                    st_done[ti] = (s0_st[b], s0_st[b].n)
                return xpu_cb

            gens = {}
            for step in range(NT + 2):
                if step < NT:
                    ti, b = step, step % 2
                    P.op("sync", lambda e, ti=ti, b=b: e.dma_start(
                        out=xin0[b][:, :], in_=x_h[ti * 128:(ti + 1) * 128, :]),
                        waits=[(s0_v, xin_free.get(ti - 2, 0)), (s_const, CONST)],
                        inc=s0_ld[b], inck=16)
                    g = emit_ln_tile_g(ti, ln0, xin0[b][:, :], arenaA, True, sems0,
                                       extra_u_cb=xpu_cb_f(ti, b),
                                       ld_wait=(s0_ld[b], s0_ld[b].n))
                    gens[ti] = g
                    next(g)
                if 0 <= step - 1 < NT:
                    next(gens[step - 1])
                if 0 <= step - 2 < NT:
                    ti2 = step - 2
                    for _ in gens[ti2]:
                        pass
                    del gens[ti2]
                    p0_tile[ti2] = sems0["tr_done"][ti2]
                    if (ti2 + 1) % 4 == 0:
                        p0_chunk[(ti2 + 1) // 4 - 1] = sems0["tr_done"][ti2]
                qkv_pump()
            XPU_ST_ALL = [(c, c.n) for c in s0_st]
            if yT_tap is not None:
                P.op("sync", lambda e: e.dma_start(out=yT_tap[:, :], in_=arenaA[:, :]),
                     waits=[(s0_s, s0_s.n), (s0_v, s0_v.n)], inc=s_tap, inck=16)

        if "qkv" in phases:
            qkv_pump()
            assert uidx[0] == len(units), "qkv units not drained"
            QKV_ST_ALL = [(c, c.n) for c in (sq_st + sq_st2)]
            stq.close()
            if "p0" in phases:
                st0.close()
            barrier("b_qkv", QKV_ST_ALL + XPU_ST_ALL + [(s_tap, s_tap.n)])

        sa_ld = sem2("sa_ld")
        sa_t = sem("sa_t")
        sa_v = sem("sa_v")
        sa_s = sem("sa_s")
        stB = ExitStack()
        if "attn" in phases or "wo" in phases:
            O_all = stB.enter_context(nc.sbuf_tensor("arenaB", [128, NT * D], BF16))
        if "attn" in phases:
            sta = ExitStack()
            NP = NH // 2
            aqp = [sb(sta, "aqp%d" % i, [128, S], BF16) for i in range(2)]
            akp = [sb(sta, "akp%d" % i, [128, S], BF16) for i in range(2)]
            avp = [sb(sta, "avp%d" % i, [128, NT * 130], BF16) for i in range(2)]
            sqp = sb(sta, "sqp", [128, 1024], BF16)
            nrow = sb(sta, "nrow", [1, S], BF16)
            kmax = sb(sta, "kmax", [1, 8], F32)
            cbias = [sb(sta, "cbias%d" % i, [128, 1], F32) for i in range(2)]
            PT = sb(sta, "PT", [128, 6 * 512], BF16)
            PTt = sb(sta, "PTt", [128, 1024], BF16)
            otsb = sb(sta, "otsb", [65, 1024], BF16)
            invl = sb(sta, "invl", [128, 8], F32)
            ones128b = sb(sta, "ones128b", [128, 8], BF16)
            onesr = sb(sta, "onesr", [1, 128], F32)

            P.op("vector", lambda e: e.memset(ones128b[:, :], 1.0))
            P.op("vector", lambda e: e.memset(onesr[:, :], 1.0))
            for i in range(2):
                P.op("vector", lambda e, i=i: e.memset(
                    avp[i][:, :].rearrange("p (kt e) -> p kt e", e=130)[:, :, 64:65], 1.0))
                P.op("vector", lambda e, i=i: e.memset(
                    avp[i][:, :].rearrange("p (kt e) -> p kt e", e=130)[:, :, 129:130], 1.0),
                    inc=sa_v)
            MS_READY = sa_v.n

            v4 = v_h.rearrange("(kt p) (h e) -> p kt h e", p=128, e=EH)

            pair_end_t = {}
            pair_end_v = {}
            pair_end_s = {}
            ld_done = {}
            bias_done = {}
            state = {
                "row_rd_v": 0,
                "pst_free": [None, None],
                "ptt_free": 0,
                "pso_free_v": 0,
                "pt_free": {},
                "pta_free": 0,
                "lhi_free_t": 0,
            }

            def emit_loads(hp):
                b = hp % 2
                w = list(QKV_ST_ALL) + [
                    (sa_v, MS_READY),
                    (sa_t, pair_end_t.get(hp - 2, 0)), (sa_v, pair_end_v.get(hp - 2, 0)),
                    (sa_s, pair_end_s.get(hp - 2, 0))]
                P.op("sync", lambda e, hp=hp, b=b: e.dma_start(
                    out=aqp[b][:, :], in_=qT_h[hp * 128:(hp + 1) * 128, :]),
                    waits=w, inc=sa_ld[b], inck=16)
                P.op("sync", lambda e, hp=hp, b=b: e.dma_start(
                    out=akp[b][:, :], in_=kT_h[hp * 128:(hp + 1) * 128, :]),
                    inc=sa_ld[b], inck=16)
                for i in range(2):
                    P.op("sync", lambda e, hp=hp, b=b, i=i: e.dma_start(
                        out=avp[b][:, :].rearrange("p (kt e) -> p kt e", e=130)
                            [:, :, i * 65:i * 65 + 64],
                        in_=v4[:, :, 2 * hp + i, :]),
                        inc=sa_ld[b], inck=16)
                ld_done[hp] = (sa_ld[b], sa_ld[b].n)

            def st_slot_use(which):
                j = state["st_rot"]
                state["st_rot"] ^= 1
                pf = state["pst_free"][j]
                return j, pf
            state["st_rot"] = 0

            def bound_prep_units(hp):
                b = hp % 2
                emit_loads(hp)
                my_ld = ld_done[hp]
                for wi, src_buf in ((0, aqp[b]), (1, akp[b])):
                    for n in range(8):
                        P.op("vector", lambda e, src_buf=src_buf, n=n: e.scalar_tensor_tensor(
                            sqp[:, (n % 2) * 512:(n % 2) * 512 + 512],
                            src_buf[:, n * 512:(n + 1) * 512], 1.0,
                            src_buf[:, n * 512:(n + 1) * 512],
                            ALU.bypass, ALU.mult),
                            waits=([my_ld, (sa_v, state["row_rd_v"])]
                                   if (n == 0 and wi == 0) else []) +
                                  [(sa_t, state.get("sq_mm", 0))], inc=sa_v)
                        sq_ready = sa_v.n
                        j, pf = st_slot_use("prep")
                        P.op("tensor", lambda e, n=n, j=j: e.matmul(
                            ps_quad[0:1, j * 1024:j * 1024 + 512],
                            ones128b[:, 0:1],
                            sqp[:, (n % 2) * 512:(n % 2) * 512 + 512],
                            start=True, stop=True),
                            waits=[(sa_v, sq_ready)] + ([pf] if pf else []), inc=sa_t)
                        state["sq_mm"] = sa_t.n
                        P.op("vector", lambda e, n=n, j=j: e.tensor_copy(
                            nrow[0:1, n * 512:(n + 1) * 512],
                            ps_quad[0:1, j * 1024:j * 1024 + 512]),
                            waits=[(sa_t, sa_t.n)], inc=sa_v)
                        state["pst_free"][j] = (sa_v, sa_v.n)
                        if n % 2 == 1:
                            yield
                    P.op("vector", lambda e, wi=wi: e.reduce_max(
                        kmax[0:1, wi:wi + 1], nrow[0:1, :], axis=AX), inc=sa_v)
                    P.wait("vector", [(sa_v, sa_v.n)])
                    yield
                state["row_rd_v"] = sa_v.n
                P.op("vector", lambda e: e.tensor_mul(kmax[0:1, 2:3], kmax[0:1, 0:1],
                                                      kmax[0:1, 1:2]), inc=sa_v)
                P.wait("vector", [(sa_v, sa_v.n)])
                t_ready = sa_v.n
                yield
                j, pf = st_slot_use("bias")
                P.op("tensor", lambda e, j=j: e.matmul(
                    ps_quad[:, j * 1024:j * 1024 + 1], onesr[0:1, :], kmax[0:1, 2:3],
                    start=True, stop=True),
                    waits=[(sa_v, t_ready)] + ([pf] if pf else []), inc=sa_t)
                bc_ready = sa_t.n
                P.wait("scalar", [(sa_t, bc_ready)])
                P.op("scalar", lambda e, b=b, j=j: e.activation(
                    out=cbias[b][:, :], in_=ps_quad[:, j * 1024:j * 1024 + 1],
                    func=AF.Sqrt, scale=0.01625625))
                P.op("scalar", lambda e, b=b: e.mul(cbias[b][:, :], cbias[b][:, :], -1.0),
                     inc=sa_s)
                state["pst_free"][j] = (sa_s, sa_s.n)
                bias_done[hp] = (sa_s, sa_s.n)
                yield

            def emit_B_quad(hp, qq, bias_rd, prev_epi):
                b = hp % 2
                nkt = 4 * qq + 4
                pt_ready = {}

                def emit_ST(kt):
                    j, pf = st_slot_use("st")
                    off = max(0, (kt - 4 * qq) * 128)
                    w0 = ([pf] if pf else [])
                    if qq == 0 and kt == 0:
                        w0 = w0 + [ld_done[hp]]
                    for i in range(2):
                        P.op("tensor", lambda e, b=b, kt=kt, qq=qq, j=j, i=i, off=off:
                             e.matmul(
                            ps_quad[:, j * 1024 + i * 512 + off:j * 1024 + (i + 1) * 512],
                            akp[b][i * 64:(i + 1) * 64, kt * 128:(kt + 1) * 128],
                            aqp[b][i * 64:(i + 1) * 64,
                                   qq * 512 + off:(qq + 1) * 512],
                            start=True, stop=True),
                            waits=(w0 if i == 0 else []),
                            inc=(sa_t if i == 1 else None))
                    return j

                def emit_exp(kt, j):
                    slot = (2 * kt) % 6
                    st_ready = sa_t.n
                    diag = (kt >= 4 * qq)
                    P.wait("scalar", [(sa_t, st_ready), bias_rd,
                                      (sa_t, state["pt_free"].get(slot, 0)),
                                      (sa_t, state["pt_free"].get(slot + 1, 0))])
                    if not diag:
                        P.op("scalar", lambda e, slot=slot, j=j, b=b: e.activation(
                            out=PT[:, slot * 512:(slot + 2) * 512],
                            in_=ps_quad[:, j * 1024:(j + 1) * 1024],
                            func=AF.Exp, scale=0.125, bias=cbias[b][:, 0:1]), inc=sa_s)
                        state["pst_free"][j] = (sa_s, sa_s.n)
                        pt_ready[2 * kt] = pt_ready[2 * kt + 1] = (sa_s, sa_s.n)
                    else:
                        off = (kt - 4 * qq) * 128
                        if off > 0:
                            P.op("scalar", lambda e, j=j, b=b, off=off: e.activation(
                                out=PTt[:, 0:1024].rearrange(
                                    "p (bk c) -> p bk c", c=512)[:, :, off:512],
                                in_=ps_quad[:, j * 1024:(j + 1) * 1024].rearrange(
                                    "p (bk c) -> p bk c", c=512)[:, :, off:512],
                                func=AF.Exp, scale=0.125, bias=cbias[b][:, 0:1]),
                                waits=[(sa_v, state["ptt_free"])], inc=sa_s)
                        else:
                            P.op("scalar", lambda e, j=j, b=b: e.activation(
                                out=PTt[:, 0:1024],
                                in_=ps_quad[:, j * 1024:(j + 1) * 1024],
                                func=AF.Exp, scale=0.125, bias=cbias[b][:, 0:1]),
                                waits=[(sa_v, state["ptt_free"])], inc=sa_s)
                        state["pst_free"][j] = (sa_s, sa_s.n)
                        vv = kt - 4 * qq
                        for i in range(2):
                            P.op("vector", lambda e, i=i, slot=slot, vv=vv: e.tensor_mul(
                                PT[:, (slot + i) * 512:(slot + i + 1) * 512],
                                PTt[:, i * 512:(i + 1) * 512],
                                c_bmask[:, 384 - 128 * vv:896 - 128 * vv]),
                                waits=[(sa_s, sa_s.n),
                                       (sa_t, state["pt_free"].get(slot + i, 0))],
                                inc=sa_v)
                            pt_ready[2 * kt + i] = (sa_v, sa_v.n)
                        state["ptt_free"] = sa_v.n

                def emit_PV(kt):
                    slot = (2 * kt) % 6
                    off = max(0, (kt - 4 * qq) * 128)
                    for i in range(2):
                        po = ps_o if i == 0 else ps_m
                        w = [pt_ready[2 * kt + i]]
                        if kt == 0:
                            w.append((sa_v, state["pso_free_v"]))
                        P.op("tensor", lambda e, b=b, kt=kt, slot=slot, i=i, po=po,
                             off=off, first=(kt == 0), last=(kt == nkt - 1): e.matmul(
                            po[0:65, off:512],
                            avp[b][:, kt * 130 + i * 65:kt * 130 + i * 65 + 65],
                            PT[:, (slot + i) * 512 + off:(slot + i + 1) * 512],
                            start=first, stop=last),
                            waits=w, inc=(sa_t if i == 1 else None))
                    state["pt_free"][slot] = sa_t.n
                    state["pt_free"][slot + 1] = sa_t.n

                jmap = {}
                for kt in range(nkt):
                    jmap[kt] = emit_ST(kt)
                    emit_exp(kt, jmap[kt])
                    if kt == 0 and prev_epi is not None:
                        prev_epi()
                    if kt >= 2:
                        emit_PV(kt - 2)
                    pump(2)
                emit_PV(nkt - 2)
                emit_PV(nkt - 1)
                pv_done = sa_t.n

                def epilogue():
                    P.wait("vector", [(sa_t, pv_done)])
                    P.op("vector", lambda e: e.tensor_copy(otsb[0:65, 0:512],
                                                           ps_o[0:65, :]))
                    P.op("vector", lambda e: e.tensor_copy(otsb[0:65, 512:1024],
                                                           ps_m[0:65, :]), inc=sa_v)
                    state["pso_free_v"] = sa_v.n
                    o_ready = sa_v.n
                    P.wait("tensor", [(sa_v, o_ready), (sa_v, state["pta_free"])])
                    for sub in range(8):
                        P.op("tensor", lambda e, sub=sub: e.transpose(
                            pt_a[:, sub * 66:sub * 66 + 65],
                            otsb[0:65, sub * 128:(sub + 1) * 128],
                            c_ident[0:65, 0:65]),
                            inc=(sa_t if sub == 7 else None))
                    tr_done = sa_t.n
                    P.wait("vector", [(sa_t, tr_done)])
                    P.op("vector", lambda e: e.reciprocal(
                        invl[:, 0:8], pt_a[:, 0:528].rearrange(
                            "p (s c) -> p s c", c=66)[:, :, 64:65]), inc=sa_v)
                    P.wait("vector", [(sa_v, sa_v.n)])
                    for sub in range(8):
                        i = sub // 4
                        ti = qq * 4 + (sub % 4)
                        h = 2 * hp + i
                        P.op("vector", lambda e, sub=sub, ti=ti, h=h: e.tensor_scalar(
                            O_all[:, ti * D + h * 64:ti * D + h * 64 + 64],
                            pt_a[:, sub * 66:sub * 66 + 64],
                            invl[:, sub:sub + 1], None,
                            ALU.mult), inc=(sa_v if sub == 7 else None))
                    state["pta_free"] = sa_v.n

                return epilogue

            gen = bound_prep_units(0)
            for _ in gen:
                pass
            GEN_BOX = [None]

            def pump(k):
                g = GEN_BOX[0]
                if g is None:
                    return
                for _ in range(k):
                    try:
                        next(g)
                    except StopIteration:
                        GEN_BOX[0] = None
                        return

            epi = None
            for hp in range(NP):
                GEN_BOX[0] = bound_prep_units(hp + 1) if hp + 1 < NP else None
                for qq in range(NQ):
                    epi = emit_B_quad(hp, qq, bias_done[hp], epi)
                while GEN_BOX[0] is not None:
                    pump(1)
                pair_end_t[hp] = sa_t.n
                pair_end_v[hp] = sa_v.n
                pair_end_s[hp] = sa_s.n
            epi()
            if oall_tap is not None:
                P.op("sync", lambda e: e.dma_start(out=oall_tap[:, :], in_=O_all[:, :]),
                     waits=[(sa_v, sa_v.n)], inc=s_tap, inck=16)
            sta.close()
            barrier("b_attn", [(s_tap, s_tap.n)])

        sw_w = sem("sw_w")
        sw_ld = sem2("sw_ld")
        sw_t = sem("sw_t")
        sw_v = sem("sw_v")
        sw_s = sem("sw_s")
        sw_st = sem2("sw_st")
        X1_ST_ALL = []
        if "wo" in phases:
            stw = ExitStack()
            wosb = sb(stw, "wosb", [128, 8 * 1024], BF16)
            ot2 = [sb(stw, "ot2_%d" % i, [128, 1024], BF16) for i in range(2)]
            otst = [sb(stw, "otst%d" % i, [128, 1024], BF16) for i in range(2)]
            xin2 = [sb(stw, "w_xin%d" % i, [128, D], F32) for i in range(2)]
            x1sb = [sb(stw, "w_x1%d" % i, [128, D], F32) for i in range(2)]
            ln2 = make_ln_bufs(stw, "w_")
            wo3 = wo_h.rearrange("(ft p) c -> p ft c", p=128)
            P.op("gpsimd", lambda e: e.dma_start(
                out=wosb[:, :].rearrange("p (ft c) -> p ft c", c=1024), in_=wo3[:, :, :]),
                inc=sw_w, inck=16)
            sems2 = {"v": sw_v, "s": sw_s, "t": sw_t, "tr_done": {}, "tp_done": {}}
            mm_free = {}
            v_done2 = {}
            st_done2 = {}
            otr_t = {}
            for ti in range(NT):
                b = ti % 2
                ptb = pt_a if ti % 2 == 0 else pt_b
                P.op("sync", lambda e, ti=ti, b=b: e.dma_start(
                    out=xin2[b][:, :], in_=xpu_h[ti * 128:(ti + 1) * 128, :]),
                    waits=list(XPU_ST_ALL) + [(sw_v, v_done2.get(ti - 2, 0))],
                    inc=sw_ld[b], inck=16)
                xin_ready = (sw_ld[b], sw_ld[b].n)
                P.op("vector", lambda e, ti=ti, b=b: e.tensor_copy(
                    otst[b][:, :], O_all[:, ti * D:(ti + 1) * D]),
                    waits=[(sw_t, otr_t.get(ti - 2, 0))], inc=sw_v)
                otst_ready = sw_v.n
                trd2 = sems2["tr_done"].get(ti - 2)
                P.wait("tensor", ([trd2] if trd2 else []) + [(sw_v, otst_ready)])
                for f in range(8):
                    P.op("tensor", lambda e, b=b, f=f, ptb=ptb: e.transpose(
                        ptb[:, f * 128:(f + 1) * 128],
                        otst[b][:, f * 128:(f + 1) * 128], c_ident[:, :]),
                        inc=(sw_t if f == 7 else None))
                otr_t[ti] = sw_t.n
                P.op("vector", lambda e, b=b, ptb=ptb: e.tensor_copy(ot2[b][:, :], ptb[:, :]),
                     waits=[(sw_t, sw_t.n)], inc=sw_v)
                ot2_ready = sw_v.n
                P.wait("tensor", [(sw_v, ot2_ready), (sw_w, 16),
                                  (sw_v, mm_free.get(ti - 2, 0))])
                pss = ((ps_a, ps_b) if ti % 2 == 0
                       else (ps_st[:, 0:512], ps_st[:, 512:1024]))
                for oh in range(2):
                    for f in range(8):
                        P.op("tensor", lambda e, b=b, oh=oh, f=f, pss=pss: e.matmul(
                            _ap(pss[oh]), ot2[b][:, f * 128:(f + 1) * 128],
                            wosb[:, f * 1024 + oh * 512:f * 1024 + oh * 512 + 512],
                            start=(f == 0), stop=(f == 7)),
                            inc=(sw_t if (oh == 1 and f == 7) else None))
                wo_done = sw_t.n
                stdp = st_done2.get(ti - 2)
                P.wait("vector", [(sw_t, wo_done), xin_ready] + ([stdp] if stdp else []))
                for oh in range(2):
                    P.op("vector", lambda e, b=b, oh=oh, pss=pss: e.tensor_add(
                        x1sb[b][:, oh * 512:(oh + 1) * 512], _ap(pss[oh]),
                        xin2[b][:, oh * 512:(oh + 1) * 512]),
                        inc=(sw_v if oh == 1 else None))
                mm_free[ti] = sw_v.n
                x1_ready = sw_v.n
                P.op("sync", lambda e, ti=ti, b=b: e.dma_start(
                    out=x1_h[ti * 128:(ti + 1) * 128, :], in_=x1sb[b][:, :]),
                    waits=[(sw_v, x1_ready)], inc=sw_st[b], inck=16)
                st_done2[ti] = (sw_st[b], sw_st[b].n)
                emit_ln_tile(ti, ln2, x1sb[b][:, :], arenaA, False, sems2)
                v_done2[ti] = sw_v.n
            X1_ST_ALL = [(c, c.n) for c in sw_st]
            if u2T_tap is not None:
                P.op("sync", lambda e: e.dma_start(out=u2T_tap[:, :], in_=arenaA[:, :]),
                     waits=[(sw_s, sw_s.n)], inc=s_tap, inck=16)
            u2_chunk = {n: sems2["tr_done"][min(4 * n + 3, NT - 1)] for n in range(8)}
            WO_PS_FREE = (sw_v, sw_v.n)
        else:
            u2_chunk = {}
            WO_PS_FREE = (None, 0)

        sfa_w = sem2("sfa_w")
        sfa_t = sem("sfa_t")
        sfa_s = sem("sfa_s")
        sfa_v = sem("sfa_v")
        sfa_st = sem2("sfa_st")
        H0_ST_ALL = []
        if "fa" in phases:
            stf = ExitStack()
            wA = [sb(stf, "faw%d" % i, [128, 2048], BF16) for i in range(2)]
            silsb = [sb(stf, "fasil%d" % i, [128, 512], BF16) for i in range(2)]
            h0row = [sb(stf, "fah0", [128, 4096], BF16)] * 2
            w1r = w1t_h.rearrange("(dt p) c -> p dt c", p=128)
            w2r = w2t_h.rearrange("(dt p) c -> p dt c", p=128)
            mm_done = {}
            st_done = {}
            sil_done = {}
            ps_free = {}
            kl = 0
            for j in range(32):
                b = j % 2
                P.op("gpsimd", lambda e, j=j, b=b: e.dma_start(
                    out=wA[b][:, 0:1024].rearrange("p (dt c) -> p dt c", c=128),
                    in_=w1r[:, :, j * 128:(j + 1) * 128]),
                    waits=[(sfa_t, mm_done.get(j - 2, 0))], inc=sfa_w[b], inck=16)
                P.op("gpsimd", lambda e, j=j, b=b: e.dma_start(
                    out=wA[b][:, 1024:2048].rearrange("p (dt c) -> p dt c", c=128),
                    in_=w2r[:, :, j * 128:(j + 1) * 128]), inc=sfa_w[b], inck=16)
                w_ready = (sfa_w[b], sfa_w[b].n)
                for n in range(8):
                    pa, pb = ((ps_a, ps_b) if n % 2 == 0
                              else (ps_st[:, 0:512], ps_st[:, 512:1024]))
                    uc = u2_chunk.get(n)
                    P.wait("tensor", [w_ready,
                                      (sfa_s, ps_free.get((kl - 2, "s"), 0)),
                                      (sfa_v, ps_free.get((kl - 2, "v"), 0))] +
                           ([uc] if uc else []) +
                           ([WO_PS_FREE] if kl < 2 and WO_PS_FREE[0] else []))
                    for dt in range(8):
                        P.op("tensor", lambda e, b=b, dt=dt, n=n, pa=pa: e.matmul(
                            _ap(pa), wA[b][:, dt * 128:(dt + 1) * 128],
                            arenaA[:, dt * S + n * 512:dt * S + n * 512 + 512],
                            start=(dt == 0), stop=(dt == 7)))
                    for dt in range(8):
                        P.op("tensor", lambda e, b=b, dt=dt, n=n, pb=pb: e.matmul(
                            _ap(pb), wA[b][:, 1024 + dt * 128:1024 + (dt + 1) * 128],
                            arenaA[:, dt * S + n * 512:dt * S + n * 512 + 512],
                            start=(dt == 0), stop=(dt == 7)),
                            inc=(sfa_t if dt == 7 else None))
                    mm_done[j] = sfa_t.n
                    P.op("scalar", lambda e, j=j, n=n, pa=pa: e.activation(
                        out=silsb[n % 2][:, :], in_=_ap(pa),
                        func=AF.Silu, scale=c_scl[:, j:j + 1]),
                        waits=[(sfa_t, sfa_t.n), (sfa_v, sil_done.get(kl - 2, 0))],
                        inc=sfa_s)
                    ps_free[(kl, "s")] = sfa_s.n
                    stdp = st_done.get(j - 1)
                    P.op("vector", lambda e, j=j, n=n, b=b, pb=pb: e.scalar_tensor_tensor(
                        h0row[b][:, n * 512:(n + 1) * 512], _ap(pb),
                        c_scl[:, 32 + j:33 + j], silsb[n % 2][:, :], ALU.mult, ALU.mult),
                        waits=[(sfa_s, sfa_s.n)] + ([stdp] if stdp else []),
                        inc=sfa_v)
                    ps_free[(kl, "v")] = sfa_v.n
                    sil_done[kl] = sfa_v.n
                    kl += 1
                P.op("sync", lambda e, j=j, b=b: e.dma_start(
                    out=h0_h[j * 128:(j + 1) * 128, :], in_=h0row[b][:, :]),
                    waits=[(sfa_v, sfa_v.n)], inc=sfa_st[b], inck=16)
                st_done[j] = (sfa_st[b], sfa_st[b].n)
            H0_ST_ALL = [(c, c.n) for c in sfa_st]
            stf.close()
        if "wo" in phases:
            stw.close()
        stB.close()
        stA.close()
        if "fa" in phases:
            barrier("b_fa", H0_ST_ALL + X1_ST_ALL + [(s_tap, s_tap.n)])
        elif "wo" in phases:
            barrier("b_wo", X1_ST_ALL + [(s_tap, s_tap.n)])

        def emit_hid_stage(tag, src_dram, wT_dram, dst_dram, scl_off, prev_st):
            s_ld = sem2(tag + "_ld")
            s_w = sem2(tag + "_w")
            s_t = sem(tag + "_t")
            s_s = sem(tag + "_s")
            s_st = sem2(tag + "_st")
            stg = ExitStack()
            hhalf = [sb(stg, tag + "hh%d" % i, [128, 16 * 2048], BF16) for i in range(2)]
            wB = [sb(stg, tag + "w%d" % i, [128, 32 * 128], BF16) for i in range(2)]
            harow = [sb(stg, tag + "r%d" % i, [128, 2048], BF16) for i in range(2)]
            srcr = src_dram.rearrange("(it p) t -> p it t", p=128)
            wr = wT_dram.rearrange("(it p) c -> p it c", p=128)
            half_mm = {}
            wj_mm = {}
            ps_free = {}
            row_st = {}
            kl = 0
            hh_ready = {}
            for H in range(2):
                hp = hhalf
                for hv in range(2):
                    P.op("sync", lambda e, H=H, hv=hv, hp=hp: e.dma_start(
                        out=hp[hv][:, :].rearrange("p (it t) -> p it t", t=2048),
                        in_=srcr[:, hv * 16:(hv + 1) * 16, H * 2048:(H + 1) * 2048]),
                        waits=(list(prev_st) + [(s_t, half_mm.get(H - 1, 0))]) if hv == 0
                        else [(s_t, half_mm.get(H - 1, 0))], inc=s_ld[hv], inck=16)
                    hh_ready[(H, hv)] = (s_ld[hv], s_ld[hv].n)
                for j in range(32):
                    jj = H * 32 + j
                    b = jj % 2
                    P.op("gpsimd", lambda e, j=j, b=b: e.dma_start(
                        out=wB[b][:, :].rearrange("p (it c) -> p it c", c=128),
                        in_=wr[:, :, j * 128:(j + 1) * 128]),
                        waits=[(s_t, wj_mm.get(jj - 2, 0))], inc=s_w[b], inck=16)
                    w_ready = (s_w[b], s_w[b].n)
                    for n in range(4):
                        ps = ps_a if kl % 2 == 0 else ps_b
                        hp = hhalf
                        P.wait("tensor", [hh_ready[(H, 0)], w_ready,
                                          (s_s, ps_free.get(kl - 2, 0))])
                        for it in range(16):
                            P.op("tensor", lambda e, b=b, it=it, n=n, ps=ps, hp=hp: e.matmul(
                                _ap(ps), wB[b][:, it * 128:(it + 1) * 128],
                                hp[0][:, it * 2048 + n * 512:it * 2048 + n * 512 + 512],
                                start=(it == 0), stop=False))
                        P.wait("tensor", [hh_ready[(H, 1)]])
                        for it in range(16, 32):
                            P.op("tensor", lambda e, b=b, it=it, n=n, ps=ps, hp=hp: e.matmul(
                                _ap(ps), wB[b][:, it * 128:(it + 1) * 128],
                                hp[1][:, (it - 16) * 2048 + n * 512:
                                       (it - 16) * 2048 + n * 512 + 512],
                                start=False, stop=(it == 31)),
                                inc=(s_t if it == 31 else None))
                        wj_mm[jj] = s_t.n
                        rsp = row_st.get(jj - 2)
                        P.op("scalar", lambda e, j=j, n=n, b=b, ps=ps: e.activation(
                            out=harow[b][:, n * 512:(n + 1) * 512], in_=_ap(ps),
                            func=AF.Gelu, scale=c_scl[:, scl_off + j:scl_off + j + 1]),
                            waits=[(s_t, s_t.n)] + ([rsp] if rsp else []),
                            inc=s_s)
                        ps_free[kl] = s_s.n
                        kl += 1
                    P.op("sync", lambda e, j=j, b=b, H=H: e.dma_start(
                        out=dst_dram[j * 128:(j + 1) * 128, H * 2048:(H + 1) * 2048],
                        in_=harow[b][:, :]),
                        waits=[(s_s, s_s.n)], inc=s_st[b], inck=16)
                    row_st[jj] = (s_st[b], s_st[b].n)
                half_mm[H] = s_t.n
            stg.close()
            return [(c, c.n) for c in s_st]

        fb_st = H0_ST_ALL
        if "fb" in phases:
            fb_st = emit_hid_stage("fb", h0_h, h1t_h, ha_h, 64, H0_ST_ALL)
            barrier("b_fb", fb_st)
        if "fc" in phases:
            fc_st = emit_hid_stage("fc", ha_h, h2t_h, hb_h, 96, fb_st)
            barrier("b_fc", fc_st)
        else:
            fc_st = fb_st

        sfd_w = sem("sfd_w")
        sfd_ld = sem2("sfd_ld")
        sfd_t = sem("sfd_t")
        sfd_v = sem("sfd_v")
        sfd_st = sem2("sfd_st")
        if "fd" in phases:
            std = ExitStack()
            c_s3b = sb(std, "c_s3b", [128, D], F32)
            s_s3b = sem("s_s3b")
            P.op("sync", lambda e: e.dma_start(out=c_s3b[:, :], in_=s3b_h[:, :]),
                 inc=s_s3b, inck=16)
            P.wait("vector", [(s_s3b, 16)])
            w3sb = sb(std, "fdw3", [128, 32 * 1024], BF16)
            hbc = [sb(std, "fdhb%d" % i, [128, 32 * 512], BF16) for i in range(2)]
            x1c = [sb(std, "fdx1%d" % i, [128, 4 * 1024], F32) for i in range(2)]
            outst = [sb(std, "fdo%d" % i, [128, D], F32) for i in range(2)]
            tmpd = [sb(std, "fdt%d" % i, [128, 512], F32) for i in range(2)]
            w3r = w3t_h.rearrange("(it p) c -> p it c", p=128)
            hbr = hb_h.rearrange("(it p) t -> p it t", p=128)
            x1r = x1_h.rearrange("(tc tt p) c -> tc p tt c", p=128, tt=4)
            P.op("gpsimd", lambda e: e.dma_start(
                out=w3sb[:, :].rearrange("p (it c) -> p it c", c=1024), in_=w3r[:, :, :]),
                inc=sfd_w, inck=16)
            tc_mm = {}
            tc_v = {}
            ps_free = {}
            tile_st = {}
            for tc in range(8):
                b = tc % 2
                P.op("sync", lambda e, tc=tc, b=b: e.dma_start(
                    out=hbc[b][:, :].rearrange("p (it t) -> p it t", t=512),
                    in_=hbr[:, :, tc * 512:(tc + 1) * 512]),
                    waits=list(fc_st) + [(sfd_t, tc_mm.get(tc - 2, 0))],
                    inc=sfd_ld[b], inck=16)
                P.op("sync", lambda e, tc=tc, b=b: e.dma_start(
                    out=x1c[b][:, :].rearrange("p (tt c) -> p tt c", c=1024),
                    in_=x1r[tc, :, :, :]),
                    waits=list(X1_ST_ALL) + [(sfd_v, tc_v.get(tc - 2, 0))],
                    inc=sfd_ld[b], inck=16)
                ld_ready = (sfd_ld[b], sfd_ld[b].n)
                for tt in range(4):
                    tile = tc * 4 + tt
                    b2 = tile % 2
                    for oh in range(2):
                        kp = tile * 2 + oh
                        ps = PS4[kp % 4]
                        P.wait("tensor", [ld_ready, (sfd_w, 16),
                                          (sfd_v, ps_free.get(kp - 4, 0))])
                        for it in range(32):
                            P.op("tensor", lambda e, b=b, it=it, tt=tt, oh=oh, ps=ps: e.matmul(
                                _ap(ps),
                                hbc[b][:, it * 512 + tt * 128:it * 512 + tt * 128 + 128],
                                w3sb[:, it * 1024 + oh * 512:it * 1024 + oh * 512 + 512],
                                start=(it == 0), stop=(it == 31)),
                                inc=(sfd_t if it == 31 else None))
                        tc_mm[tc] = sfd_t.n
                        tsp = tile_st.get(tile - 2)
                        P.wait("vector", [(sfd_t, sfd_t.n)] + ([tsp] if tsp else []))
                        P.op("vector", lambda e, oh=oh, ps=ps, b2=b2: e.tensor_mul(
                            tmpd[b2][:, :], _ap(ps), c_s3b[:, oh * 512:(oh + 1) * 512]))
                        P.op("vector", lambda e, oh=oh, b=b, b2=b2, tt=tt: e.tensor_add(
                            outst[b2][:, oh * 512:(oh + 1) * 512], tmpd[b2][:, :],
                            x1c[b][:, tt * 1024 + oh * 512:tt * 1024 + oh * 512 + 512]),
                            inc=sfd_v)
                        ps_free[kp] = sfd_v.n
                    tc_v[tc] = sfd_v.n
                    P.op("sync", lambda e, tile=tile, b2=b2: e.dma_start(
                        out=out_h[tile * 128:(tile + 1) * 128, :], in_=outst[b2][:, :]),
                        waits=[(sfd_v, sfd_v.n)], inc=sfd_st[b2], inck=16)
                    tile_st[tile] = (sfd_st[b2], sfd_st[b2].n)
            std.close()

        fw = [(c, c.n) for c in ([s_tap] + s0_st + sq_st + sq_st2 + sw_st + sfa_st + sfd_st)]
        P.wait("sync", fw)

        with nc.Block() as block:
            for ename in ENGS:
                fns = P.q[ename]
                getattr(block, ename)(lambda e, fns=fns: [f(e) for f in fns])

    return nc


_NC_CACHE = {}


def _get_nc(upto="fd", taps=()):
    key = (upto, tuple(taps))
    if key not in _NC_CACHE:
        _NC_CACHE[key] = build_nc(upto, taps)
    return _NC_CACHE[key]


def make_in_maps(inputs, n_cores=8):
    f32 = np.float32
    bf16 = ml_dtypes.bfloat16
    wq = np.asarray(inputs["L0_attn_wq"], f32).transpose(1, 0, 2).reshape(D, D)
    wk = np.asarray(inputs["L0_attn_wk"], f32).transpose(1, 0, 2).reshape(D, D)
    wv = np.asarray(inputs["L0_attn_wv"], f32).transpose(1, 0, 2).reshape(D, D)
    wqkv = np.ascontiguousarray(np.stack([wq, wk, wv]))
    wo = np.ascontiguousarray(np.asarray(inputs["L0_attn_wo"], f32))
    w1t = np.ascontiguousarray(np.asarray(inputs["L0_ffn_w1_w"], f32).T)
    w2t = np.ascontiguousarray(np.asarray(inputs["L0_ffn_w2_w"], f32).T)
    h1t = np.ascontiguousarray(np.asarray(inputs["L0_ffn_h1_w"], f32).T)
    h2t = np.ascontiguousarray(np.asarray(inputs["L0_ffn_h2_w"], f32).T)
    w3t = np.ascontiguousarray(np.asarray(inputs["L0_ffn_w3_w"], f32).T)
    lnw = np.asarray(inputs["L0_attn_ln_w"], f32).reshape(8, 128).T
    lnb = np.asarray(inputs["L0_attn_ln_b"], f32).reshape(8, 128).T
    lnwb = np.ascontiguousarray(np.concatenate([lnw, lnb], axis=1))
    scl = np.ascontiguousarray(np.concatenate(
        [np.asarray(inputs[k], f32).reshape(32, 128).T
         for k in ("L0_ffn_w1_s", "L0_ffn_w2_s", "L0_ffn_h1_s", "L0_ffn_h2_s")], axis=1))
    s3b = np.ascontiguousarray(np.broadcast_to(
        np.asarray(inputs["L0_ffn_w3_s"], f32), (128, D)))
    ident = np.eye(128, dtype=bf16)
    nident = (-np.eye(128, dtype=np.float32)).astype(bf16)
    kk = np.arange(128)[:, None]
    xx2 = np.arange(896)[None, :] - 384
    bmask = np.ascontiguousarray((kk <= xx2).astype(np.float32).astype(bf16))
    konst = np.zeros((128, 8), f32)
    konst[:, :6] = np.array([1e-5, -0.5, 0.125, -1.0, 1.0, 1.0 / 1024.0], f32)

    x = np.asarray(inputs["L0_attn_x"], f32)
    shared = dict(wqkv=wqkv, wo=wo, w1t=w1t, w2t=w2t, h1t=h1t, h2t=h2t, w3t=w3t,
                  lnwb=lnwb, scl=scl, s3b=s3b, ident=ident, nident=nident, bmask=bmask,
                  konst=konst)
    in_maps = []
    for c in range(n_cores):
        m = dict(shared)
        m["x"] = np.ascontiguousarray(x[c])
        in_maps.append(m)
    return in_maps


def kernel(**inputs):
    nc = _get_nc("fd", ())
    in_maps = make_in_maps(inputs)
    res = run_bass_kernel_spmd(nc, in_maps, core_ids=list(range(8)))
    return np.stack([np.asarray(r["out"]) for r in res.results])


# version 18 (final)
# speedup vs baseline: 1.0456x; 1.0456x over previous
import numpy as np
import ml_dtypes
from contextlib import ExitStack

import concourse.bass as bass
import concourse.mybir as mybir
from concourse.bass_utils import run_bass_kernel_spmd

F32 = mybir.dt.float32
BF16 = mybir.dt.bfloat16
AF = mybir.ActivationFunctionType
ALU = mybir.AluOpType
AX = mybir.AxisListType.X

S = 4096
D = 1024
HID = 4096
NH = 16
EH = 64
NT = S // 128
NQ = S // 512
EPS = 1e-5

ENGS = ("sync", "gpsimd", "tensor", "vector", "scalar")
ALL_PHASES = ("p0", "qkv", "attn", "wo", "fa", "fb", "fc", "fd")


class Ch:
    __slots__ = ("h", "n", "name")

    def __init__(self, h, name=""):
        self.h = h
        self.n = 0
        self.name = name


class Prog:
    def __init__(self, paranoid=None):
        self.q = {e: [] for e in ENGS}
        self.paranoid = paranoid
        self.last_edge = {}

    def op(self, eng, fn, waits=(), inc=None, inck=1, edge=True):
        w = tuple((c.h, v) for (c, v) in waits if c is not None and v > 0)
        tgt = None
        if inc is not None:
            inc.n += inck
            tgt = (inc.h, inck)
        pwait = pinc = None
        if self.paranoid is not None and eng in self.paranoid:
            pwait = self.last_edge.get(eng)
            if not edge:
                pass
            elif tgt is not None:
                self.last_edge[eng] = (inc.h, inc.n)
            else:
                pc = self.paranoid[eng]
                pc.n += 1
                pinc = (pc.h, 1)
                self.last_edge[eng] = (pc.h, pc.n)

        def run(e, fn=fn, w=w, tgt=tgt, pwait=pwait, pinc=pinc):
            if pwait is not None:
                e.wait_ge(*pwait)
            for h, v in w:
                e.wait_ge(h, v)
            r = fn(e)
            if tgt is not None:
                r.then_inc(*tgt)
            if pinc is not None:
                r.then_inc(pinc[0], pinc[1], skip_validation=True)

        self.q[eng].append(run)
        return inc.n if inc is not None else 0

    def wait(self, eng, waits):
        w = tuple((c.h, v) for (c, v) in waits if c is not None and v > 0)
        if not w:
            return

        def run(e, w=w):
            for h, v in w:
                e.wait_ge(h, v)

        self.q[eng].append(run)


def _ap(x):
    return x if isinstance(x, bass.AP) else x[:, :]


def build_nc(upto="fd", taps=(), paranoid=False):
    nphase = ALL_PHASES.index(upto) + 1
    phases = set(ALL_PHASES[:nphase])
    nc = bass.Bass("TRN2", target_bir_lowering=False, debug=False)

    x_h = nc.dram_tensor("x", [S, D], F32, kind="ExternalInput").ap()
    wqkv_h = nc.dram_tensor("wqkv", [3, D, D], F32, kind="ExternalInput").ap()
    wo_h = nc.dram_tensor("wo", [D, D], F32, kind="ExternalInput").ap()
    w1t_h = nc.dram_tensor("w1t", [D, HID], F32, kind="ExternalInput").ap()
    w2t_h = nc.dram_tensor("w2t", [D, HID], F32, kind="ExternalInput").ap()
    h1t_h = nc.dram_tensor("h1t", [HID, HID], F32, kind="ExternalInput").ap()
    h2t_h = nc.dram_tensor("h2t", [HID, HID], F32, kind="ExternalInput").ap()
    w3t_h = nc.dram_tensor("w3t", [HID, D], F32, kind="ExternalInput").ap()
    lnwb_h = nc.dram_tensor("lnwb", [128, 16], F32, kind="ExternalInput").ap()
    scl_h = nc.dram_tensor("scl", [128, 128], F32, kind="ExternalInput").ap()
    s3b_h = nc.dram_tensor("s3b", [128, D], F32, kind="ExternalInput").ap()
    ident_h = nc.dram_tensor("ident", [128, 128], BF16, kind="ExternalInput").ap()
    nident_h = nc.dram_tensor("nident", [128, 128], BF16, kind="ExternalInput").ap()
    bmask_h = nc.dram_tensor("bmask", [128, 896], BF16, kind="ExternalInput").ap()
    konst_h = nc.dram_tensor("konst", [128, 8], F32, kind="ExternalInput").ap()

    out_h = nc.dram_tensor("out", [S, D], F32, kind="ExternalOutput").ap()

    def scratch(name, shape, dtype):
        kind = "ExternalOutput" if name in taps else "Internal"
        return nc.dram_tensor(name, shape, dtype, kind=kind).ap()

    xpu_h = scratch("t_xpu", [S, D], F32)
    x1_h = scratch("t_x1", [S, D], F32)
    qT_h = scratch("t_qT", [NH * EH, S], BF16)
    kT_h = scratch("t_kT", [NH * EH, S], BF16)
    v_h = scratch("t_v", [S, D], BF16)
    h0_h = scratch("t_h0", [HID, S], BF16)
    ha_h = scratch("t_ha", [HID, S], BF16)
    hb_h = scratch("t_hb", [HID, S], BF16)
    yT_tap = scratch("t_yT", [128, 8 * S], BF16) if "t_yT" in taps else None
    u2T_tap = scratch("t_u2T", [128, 8 * S], BF16) if "t_u2T" in taps else None
    oall_tap = scratch("t_oall", [128, NT * D], BF16) if "t_oall" in taps else None

    st = ExitStack()
    final_waits = []

    with st:
        def sem(name):
            return Ch(st.enter_context(nc.semaphore(name)), name)

        def sem2(name):
            return [sem(name + "a"), sem(name + "b")]

        def sb(stk, name, shape, dtype):
            return stk.enter_context(nc.sbuf_tensor(name, shape, dtype))

        if paranoid:
            P_par = {e: sem("par_" + e) for e in ("scalar", "vector", "gpsimd")}
        else:
            P_par = None
        P = Prog(P_par)
        s_tap = sem("s_tap")

        ps_quad = st.enter_context(nc.psum_tensor("ps_quad", [128, 2048], F32))
        ps_a = ps_quad[:, 0:512]
        ps_b = ps_quad[:, 512:1024]
        ps_st = ps_quad[:, 1024:2048]
        ps_o = st.enter_context(nc.psum_tensor("ps_o", [128, 512], F32))
        ps_m = st.enter_context(nc.psum_tensor("ps_m", [128, 512], F32))
        pt_a = st.enter_context(nc.psum_tensor("pt_a", [128, 1024], BF16))
        pt_b = st.enter_context(nc.psum_tensor("pt_b", [128, 1024], BF16))
        PS4 = [ps_quad[:, i * 512:(i + 1) * 512] for i in range(4)]

        c_ident = sb(st, "c_ident", [128, 128], BF16)
        c_nident = sb(st, "c_nident", [128, 128], BF16)
        c_bmask = sb(st, "c_bmask", [128, 896], BF16)
        c_lnwb = sb(st, "c_lnwb", [128, 16], F32)
        c_scl = sb(st, "c_scl", [128, 128], F32)
        c_ones = sb(st, "c_ones", [1, 8], F32)
        c_konst = sb(st, "c_konst", [128, 8], F32)
        KONSTS = (EPS, -0.5, 0.125, -1.0, 1.0, 1.0 / D)
        for ki, kv in enumerate(KONSTS):
            nc.const_aps.register(F32, kv, c_konst[:, ki:ki + 1])

        s_const = sem("s_const")
        for dst, src in ((c_ident, ident_h), (c_nident, nident_h), (c_bmask, bmask_h),
                         (c_lnwb, lnwb_h), (c_scl, scl_h),
                         (c_konst, konst_h)):
            P.op("sync", lambda e, d=dst, s=src: e.dma_start(out=d[:, :], in_=s[:, :]),
                 inc=s_const, inck=16)
        P.op("vector", lambda e: e.memset(c_ones[:, :], 1.0), inc=s_const)
        CONST = s_const.n

        def barrier(name, drains=()):
            c = sem(name)
            P.wait("sync", list(drains))
            for e in ENGS:
                P.op(e, lambda eng: eng.nop(), inc=c, edge=False)
            for e in ENGS:
                P.wait(e, [(c, len(ENGS))])

        def emit_ln_tile_g(ti, bufs, src_sb, yT_dst, affine, sems, extra_u_cb=None,
                           ld_wait=None):
            b = ti % 2
            s_v, s_s, s_t = sems["v"], sems["s"], sems["t"]
            ptb = pt_a if (ti % 2 == 0) else pt_b
            P.op("vector", lambda e, b=b: e.reduce_sum(bufs["mv"][b][:, 0:1], src_sb[:, :],
                                                       axis=AX),
                 waits=([ld_wait] if ld_wait else []))
            P.op("vector", lambda e, b=b: e.scalar_tensor_tensor(
                bufs["sq"][b][:, :], src_sb[:, :], 1.0, src_sb[:, :],
                ALU.bypass, ALU.mult, accum_out=bufs["mv"][b][:, 1:2]), inc=s_v)
            mv_ready = s_v.n
            P.wait("scalar", [(s_v, mv_ready)])
            P.op("scalar", lambda e, b=b: e.mul(bufs["mn"][b][:, :],
                                                bufs["mv"][b][:, 0:1], 1.0 / D))
            P.op("scalar", lambda e, b=b: e.activation(out=bufs["msq"][b][:, :],
                                                       in_=bufs["mn"][b][:, :],
                                                       func=AF.Square))
            P.op("scalar", lambda e, b=b: e.mul(bufs["var"][b][:, :],
                                                bufs["msq"][b][:, :], -1.0))
            P.op("scalar", lambda e, b=b: e.add(bufs["var"][b][:, :],
                                                bufs["var"][b][:, :], EPS))
            P.op("scalar", lambda e, b=b: e.activation(out=bufs["tl"][b][:, :],
                                                       in_=bufs["mv"][b][:, 1:2],
                                                       func=AF.Ln, scale=1.0 / D,
                                                       bias=bufs["var"][b][:, :]))
            P.op("scalar", lambda e, b=b: e.activation(out=bufs["rs"][b][:, :],
                                                       in_=bufs["tl"][b][:, :],
                                                       func=AF.Exp, scale=-0.5), inc=s_s)
            rstd_ready = s_s.n
            yield
            P.wait("vector", [(s_s, rstd_ready)])
            P.op("vector", lambda e, b=b: e.scalar_tensor_tensor(
                bufs["nm"][b][:, :], bufs["mn"][b][:, :], -1.0, bufs["rs"][b][:, :],
                ALU.mult, ALU.mult), inc=s_v)
            P.wait("vector", [(s_v, s_v.n)])
            P.op("vector", lambda e, b=b: e.tensor_scalar(
                bufs["u"][b][:, :], src_sb[:, :], bufs["rs"][b][:, 0:1],
                bufs["nm"][b][:, 0:1], ALU.mult, ALU.add), inc=s_v)
            u_ready = s_v.n
            if extra_u_cb is not None:
                extra_u_cb(bufs["u"][b])
            P.op("scalar", lambda e, b=b: e.copy(out=bufs["ubf"][b][:, :],
                                                 in_=bufs["u"][b][:, :]),
                 waits=[(s_t, sems["tp_done"].get(ti - 2, 0)), (s_v, u_ready)], inc=s_s)
            ubf_ready = (s_s, s_s.n)
            trd = sems["tr_done"].get(ti - 2)
            P.wait("tensor", [ubf_ready,
                              (trd[0] if trd else None, trd[1] if trd else 0),
                              (s_const, CONST)])
            for f in range(8):
                P.op("tensor", lambda e, b=b, f=f, ptb=ptb: e.transpose(
                    ptb[:, f * 128:(f + 1) * 128], bufs["ubf"][b][:, f * 128:(f + 1) * 128],
                    c_ident[:, :]), inc=(s_t if f == 7 else None))
            sems["tp_done"][ti] = s_t.n
            tr_ready = s_t.n
            yield
            if affine:
                P.wait("vector", [(s_t, tr_ready)])
                for f in range(8):
                    P.op("vector", lambda e, f=f, ti=ti, ptb=ptb: e.tensor_scalar(
                        yT_dst[:, f * S + ti * 128:f * S + ti * 128 + 128],
                        ptb[:, f * 128:(f + 1) * 128],
                        c_lnwb[:, f:f + 1], c_lnwb[:, 8 + f:9 + f], ALU.mult, ALU.add),
                        inc=(s_v if f == 7 else None))
                sems["tr_done"][ti] = (s_v, s_v.n)
            else:
                P.wait("scalar", [(s_t, tr_ready)])
                ydst = yT_dst[:, 0:8 * S].rearrange("p (f t) -> p f t", t=S)
                P.op("scalar", lambda e, ti=ti, ptb=ptb, ydst=ydst: e.copy(
                    out=ydst[:, :, ti * 128:ti * 128 + 128],
                    in_=ptb[:, :].rearrange("p (f t) -> p f t", t=128)),
                    inc=s_s)
                sems["tr_done"][ti] = (s_s, s_s.n)

        def emit_ln_tile(*a, **kw):
            for _ in emit_ln_tile_g(*a, **kw):
                pass

        def make_ln_bufs(stk, tag):
            return {
                "sq": [sb(stk, tag + "sq", [128, D], F32)] * 2,
                "mn": [sb(stk, tag + "mn%d" % i, [128, 1], F32) for i in range(2)],
                "msq": [sb(stk, tag + "ms%d" % i, [128, 1], F32) for i in range(2)],
                "var": [sb(stk, tag + "va%d" % i, [128, 1], F32) for i in range(2)],
                "mv": [sb(stk, tag + "mv%d" % i, [128, 2], F32) for i in range(2)],
                "rs": [sb(stk, tag + "rs%d" % i, [128, 1], F32) for i in range(2)],
                "nm": [sb(stk, tag + "nm%d" % i, [128, 1], F32) for i in range(2)],
                "tl": [sb(stk, tag + "tl%d" % i, [128, 1], F32) for i in range(2)],
                "u": [sb(stk, tag + "u%d" % i, [128, D], F32) for i in range(2)],
                "ubf": [sb(stk, tag + "ubf%d" % i, [128, D], BF16) for i in range(2)],
            }

        stA = ExitStack()
        arenaA = stA.enter_context(nc.sbuf_tensor("arenaA", [128, 8 * S], BF16))

        s0_ld = sem2("s0_ld")
        s0_ld2 = sem2("s0_ld2")
        s0_v = sem("s0_v")
        s0_s = sem("s0_s")
        s0_t = sem("s0_t")
        s0_st = sem2("s0_st")
        XPU_ST_ALL = []

        p0_chunk = {}
        p0_tile = {}
        if "p0" in phases:
            st0 = ExitStack()
            ln0 = make_ln_bufs(st0, "p0")
            xin0 = [sb(st0, "p0xin%d" % i, [128, D], F32) for i in range(4)]
            xpu0 = [sb(st0, "p0xpu%d" % i, [128, D], F32) for i in range(2)]
            sems0 = {"v": s0_v, "s": s0_s, "t": s0_t, "tr_done": {}, "tp_done": {}}
            xin_free = {}
            st_done = {}

        sq_w = sem2("sq_w")
        sq_wv = sem("sq_wv")
        sq_t = sem("sq_t")
        sq_s = sem("sq_s")
        sq_s2 = sem("sq_s2")
        sq_st = sem2("sq_st")
        sq_st2 = sem2("sq_st2")
        QKV_ST_ALL = []
        if "qkv" in phases:
            stq = ExitStack()
            wqk_sb = sb(stq, "qkw", [128, 16384], BF16)
            wvbuf = sb(stq, "qkwv", [128, 8 * 1024], BF16)
            ev = [sb(stq, "qkev%d" % i, [128, 512], BF16) for i in range(2)]
            vst = [sb(stq, "qkvst%d" % i, [128, 1024], BF16) for i in range(2)]
            wq3 = wqkv_h.rearrange("m (dt p) c -> m p dt c", p=128)
            for m in range(2):
                for ot in range(8):
                    idx = m * 8 + ot
                    P.op("gpsimd", lambda e, m=m, ot=ot, idx=idx: e.dma_start(
                        out=wqk_sb[:, idx * 1024:(idx + 1) * 1024].rearrange(
                            "p (dt c) -> p dt c", c=128),
                        in_=wq3[m, :, :, ot * 128:(ot + 1) * 128]),
                        inc=sq_w[m], inck=16)
            P.op("gpsimd", lambda e: e.dma_start(
                out=wvbuf[:, :].rearrange("p (dt c) -> p dt c", c=1024),
                in_=wq3[2, :, :, :]), inc=sq_wv, inck=16)
            wv_ready = sq_wv.n
            qkv_state = {"kpsum": 0, "kev": 0}
            ps_free = {}
            st_ev = {}
            vst_done = {}

            def emit_qk_unit(c, idx):
                m = idx // 8
                ks = qkv_state["kpsum"]
                qkv_state["kpsum"] += 1
                ps = PS4[ks % 4]
                pf = ps_free.get(ks - 4)
                P.wait("tensor", [(sq_w[m], 128), p0_chunk[c]] + ([pf] if pf else []))
                for dt in range(8):
                    P.op("tensor", lambda e, idx=idx, dt=dt, c=c, ps=ps: e.matmul(
                        _ap(ps),
                        wqk_sb[:, idx * 1024 + dt * 128:idx * 1024 + (dt + 1) * 128],
                        arenaA[:, dt * S + c * 512:dt * S + c * 512 + 512],
                        start=(dt == 0), stop=(dt == 7)),
                        inc=(sq_t if dt == 7 else None))
                kev = qkv_state["kev"]
                qkv_state["kev"] += 1
                b2 = kev % 2
                sev = st_ev.get(kev - 2)
                cch = sq_s if b2 == 0 else sq_s2
                if b2 == 0:
                    P.op("scalar", lambda e, b2=b2, ps=ps: e.copy(
                        out=ev[b2][:, :], in_=_ap(ps)),
                        waits=[(sq_t, sq_t.n)] + ([sev] if sev else []), inc=cch)
                else:
                    P.op("vector", lambda e, b2=b2, ps=ps: e.tensor_copy(
                        ev[b2][:, :], _ap(ps)),
                        waits=[(sq_t, sq_t.n)] + ([sev] if sev else []), inc=cch)
                ps_free[ks] = (cch, cch.n)
                dst = qT_h if m == 0 else kT_h
                ot = idx % 8
                P.op("scalar" if b2 == 0 else "gpsimd",
                     lambda e, dst=dst, ot=ot, c=c, b2=b2: e.dma_start(
                    out=dst[ot * 128:(ot + 1) * 128, c * 512:(c + 1) * 512],
                    in_=ev[b2][:, :]),
                    waits=[(cch, cch.n)], inc=sq_st[b2], inck=16)
                st_ev[kev] = (sq_st[b2], sq_st[b2].n)

            def emit_v_unit(tt, eh):
                b = tt % 2
                ks = qkv_state["kpsum"]
                qkv_state["kpsum"] += 1
                ps = PS4[ks % 4]
                pf = ps_free.get(ks - 4)
                P.wait("tensor", [(sq_wv, wv_ready), p0_tile[tt]] +
                       ([pf] if pf else []))
                for dt in range(8):
                    P.op("tensor", lambda e, tt=tt, dt=dt, eh=eh, ps=ps: e.matmul(
                        _ap(ps), arenaA[:, dt * S + tt * 128:dt * S + tt * 128 + 128],
                        wvbuf[:, dt * 1024 + eh * 512:dt * 1024 + eh * 512 + 512],
                        start=(dt == 0), stop=(dt == 7)),
                        inc=(sq_t if dt == 7 else None))
                cch = sq_s if eh == 0 else sq_s2
                vstp = vst_done.get(tt - 2)
                if eh == 0:
                    P.op("scalar", lambda e, b=b, eh=eh, ps=ps: e.copy(
                        out=vst[b][:, eh * 512:(eh + 1) * 512], in_=_ap(ps)),
                        waits=[(sq_t, sq_t.n)] + ([vstp] if vstp else []), inc=cch)
                else:
                    P.op("vector", lambda e, b=b, eh=eh, ps=ps: e.tensor_copy(
                        vst[b][:, eh * 512:(eh + 1) * 512], _ap(ps)),
                        waits=[(sq_t, sq_t.n)] + ([vstp] if vstp else []), inc=cch)
                ps_free[ks] = (cch, cch.n)
                if eh == 1:
                    P.op("sync", lambda e, b=b, tt=tt: e.dma_start(
                        out=v_h[tt * 128:(tt + 1) * 128, :], in_=vst[b][:, :]),
                        waits=[(sq_s, sq_s.n), (sq_s2, sq_s2.n)], inc=sq_st2[b], inck=16)
                    vst_done[tt] = (sq_st2[b], sq_st2[b].n)

            def qkv_unit_list():
                out = []
                for c in range(8):
                    for idx in range(16):
                        out.append(("qk", c, idx))
                    for tt in range(4 * c, 4 * c + 4):
                        for eh in range(2):
                            out.append(("v", tt, eh))
                return out
            units = qkv_unit_list()
            uidx = [0]

            def qkv_pump():
                while uidx[0] < len(units):
                    kind, a1, a2 = units[uidx[0]]
                    if kind == "qk" and a1 not in p0_chunk:
                        return
                    if kind == "v" and a1 not in p0_tile:
                        return
                    (emit_qk_unit if kind == "qk" else emit_v_unit)(a1, a2)
                    uidx[0] += 1
        else:
            def qkv_pump():
                pass

        if "p0" in phases:
            def xpu_cb_f(ti, b, b4):
                def xpu_cb(u_f32):
                    std_prev = st_done.get(ti - 2)
                    P.op("vector", lambda e, b=b, b4=b4, u_f32=u_f32: e.tensor_add(
                        xpu0[b][:, :], xin0[b4][:, :], u_f32[:, :]),
                        waits=[std_prev] if std_prev else [], inc=s0_v)
                    xin_free[ti] = s0_v.n
                    P.op("scalar", lambda e, ti=ti, b=b: e.dma_start(
                        out=xpu_h[ti * 128:(ti + 1) * 128, :], in_=xpu0[b][:, :]),
                        waits=[(s0_v, xin_free[ti])], inc=s0_st[b], inck=16)
                    st_done[ti] = (s0_st[b], s0_st[b].n)
                return xpu_cb

            gens = {}
            for step in range(NT + 2):
                if step < NT:
                    ti, b = step, step % 2
                    b4 = step % 4
                    ldch = (s0_ld + s0_ld2)[b4]
                    P.op("sync", lambda e, ti=ti, b4=b4: e.dma_start(
                        out=xin0[b4][:, :], in_=x_h[ti * 128:(ti + 1) * 128, :]),
                        waits=[(s0_v, xin_free.get(ti - 4, 0)), (s_const, CONST)],
                        inc=ldch, inck=16)
                    g = emit_ln_tile_g(ti, ln0, xin0[b4][:, :], arenaA, True, sems0,
                                       extra_u_cb=xpu_cb_f(ti, b, b4),
                                       ld_wait=(ldch, ldch.n))
                    gens[ti] = g
                    next(g)
                if 0 <= step - 1 < NT:
                    next(gens[step - 1])
                if 0 <= step - 2 < NT:
                    ti2 = step - 2
                    for _ in gens[ti2]:
                        pass
                    del gens[ti2]
                    p0_tile[ti2] = sems0["tr_done"][ti2]
                    if (ti2 + 1) % 4 == 0:
                        p0_chunk[(ti2 + 1) // 4 - 1] = sems0["tr_done"][ti2]
                qkv_pump()
            XPU_ST_ALL = [(c, c.n) for c in s0_st]
            if yT_tap is not None:
                P.op("sync", lambda e: e.dma_start(out=yT_tap[:, :], in_=arenaA[:, :]),
                     waits=[(s0_s, s0_s.n), (s0_v, s0_v.n)], inc=s_tap, inck=16)

        if "qkv" in phases:
            qkv_pump()
            assert uidx[0] == len(units), "qkv units not drained"
            QKV_ST_ALL = [(c, c.n) for c in (sq_st + sq_st2)]
            stq.close()
            if "p0" in phases:
                st0.close()
            barrier("b_qkv", QKV_ST_ALL + XPU_ST_ALL + [(s_tap, s_tap.n)])

        sa_ld = sem2("sa_ld")
        sa_t = sem("sa_t")
        sa_v = sem("sa_v")
        sa_s = sem("sa_s")
        stB = ExitStack()
        if "attn" in phases or "wo" in phases:
            O_all = stB.enter_context(nc.sbuf_tensor("arenaB", [128, NT * D], BF16))
        if "attn" in phases:
            sta = ExitStack()
            NP = NH // 2
            aqp = [sb(sta, "aqp%d" % i, [128, S], BF16) for i in range(2)]
            akp = [sb(sta, "akp%d" % i, [128, S], BF16) for i in range(2)]
            avp = [sb(sta, "avp%d" % i, [128, NT * 130], BF16) for i in range(2)]
            sqp = sb(sta, "sqp", [128, 1024], BF16)
            nrow = sb(sta, "nrow", [1, S], BF16)
            kmax = sb(sta, "kmax", [1, 8], F32)
            cbias = [sb(sta, "cbias%d" % i, [128, 1], F32) for i in range(2)]
            PT = sb(sta, "PT", [128, 6 * 512], BF16)
            PTt = sb(sta, "PTt", [128, 1024], BF16)
            otsb = sb(sta, "otsb", [65, 1024], BF16)
            invl = sb(sta, "invl", [128, 8], F32)
            ones128b = sb(sta, "ones128b", [128, 8], BF16)
            onesr = sb(sta, "onesr", [1, 128], F32)

            P.op("vector", lambda e: e.memset(ones128b[:, :], 1.0))
            P.op("vector", lambda e: e.memset(onesr[:, :], 1.0))
            for i in range(2):
                P.op("vector", lambda e, i=i: e.memset(
                    avp[i][:, :].rearrange("p (kt e) -> p kt e", e=130)[:, :, 64:65], 1.0))
                P.op("vector", lambda e, i=i: e.memset(
                    avp[i][:, :].rearrange("p (kt e) -> p kt e", e=130)[:, :, 129:130], 1.0),
                    inc=sa_v)
            MS_READY = sa_v.n

            v4 = v_h.rearrange("(kt p) (h e) -> p kt h e", p=128, e=EH)

            pair_end_t = {}
            pair_end_v = {}
            pair_end_s = {}
            ld_done = {}
            bias_done = {}
            state = {
                "row_rd_v": 0,
                "pst_free": [None, None],
                "ptt_free": 0,
                "pso_free_v": 0,
                "pt_free": {},
                "pta_free": 0,
                "lhi_free_t": 0,
            }

            def emit_loads(hp):
                b = hp % 2
                w = list(QKV_ST_ALL) + [
                    (sa_v, MS_READY),
                    (sa_t, pair_end_t.get(hp - 2, 0)), (sa_v, pair_end_v.get(hp - 2, 0)),
                    (sa_s, pair_end_s.get(hp - 2, 0))]
                P.op("sync", lambda e, hp=hp, b=b: e.dma_start(
                    out=aqp[b][:, :], in_=qT_h[hp * 128:(hp + 1) * 128, :]),
                    waits=w, inc=sa_ld[b], inck=16)
                P.op("sync", lambda e, hp=hp, b=b: e.dma_start(
                    out=akp[b][:, :], in_=kT_h[hp * 128:(hp + 1) * 128, :]),
                    inc=sa_ld[b], inck=16)
                for i in range(2):
                    P.op("sync", lambda e, hp=hp, b=b, i=i: e.dma_start(
                        out=avp[b][:, :].rearrange("p (kt e) -> p kt e", e=130)
                            [:, :, i * 65:i * 65 + 64],
                        in_=v4[:, :, 2 * hp + i, :]),
                        inc=sa_ld[b], inck=16)
                ld_done[hp] = (sa_ld[b], sa_ld[b].n)

            def st_slot_use(which):
                j = state["st_rot"]
                state["st_rot"] ^= 1
                pf = state["pst_free"][j]
                return j, pf
            state["st_rot"] = 0

            def bound_prep_units(hp):
                b = hp % 2
                emit_loads(hp)
                my_ld = ld_done[hp]
                for wi, src_buf in ((0, aqp[b]), (1, akp[b])):
                    for n in range(8):
                        P.op("vector", lambda e, src_buf=src_buf, n=n: e.scalar_tensor_tensor(
                            sqp[:, (n % 2) * 512:(n % 2) * 512 + 512],
                            src_buf[:, n * 512:(n + 1) * 512], 1.0,
                            src_buf[:, n * 512:(n + 1) * 512],
                            ALU.bypass, ALU.mult),
                            waits=([my_ld, (sa_v, state["row_rd_v"])]
                                   if (n == 0 and wi == 0) else []) +
                                  [(sa_t, state.get("sq_mm", 0))], inc=sa_v)
                        sq_ready = sa_v.n
                        j, pf = st_slot_use("prep")
                        P.op("tensor", lambda e, n=n, j=j: e.matmul(
                            ps_quad[0:1, j * 1024:j * 1024 + 512],
                            ones128b[:, 0:1],
                            sqp[:, (n % 2) * 512:(n % 2) * 512 + 512],
                            start=True, stop=True),
                            waits=[(sa_v, sq_ready)] + ([pf] if pf else []), inc=sa_t)
                        state["sq_mm"] = sa_t.n
                        P.op("vector", lambda e, n=n, j=j: e.tensor_copy(
                            nrow[0:1, n * 512:(n + 1) * 512],
                            ps_quad[0:1, j * 1024:j * 1024 + 512]),
                            waits=[(sa_t, sa_t.n)], inc=sa_v)
                        state["pst_free"][j] = (sa_v, sa_v.n)
                        if n % 2 == 1:
                            yield
                    P.op("vector", lambda e, wi=wi: e.reduce_max(
                        kmax[0:1, wi:wi + 1], nrow[0:1, :], axis=AX), inc=sa_v)
                    P.wait("vector", [(sa_v, sa_v.n)])
                    yield
                state["row_rd_v"] = sa_v.n
                P.op("vector", lambda e: e.tensor_mul(kmax[0:1, 2:3], kmax[0:1, 0:1],
                                                      kmax[0:1, 1:2]), inc=sa_v)
                P.wait("vector", [(sa_v, sa_v.n)])
                t_ready = sa_v.n
                yield
                j, pf = st_slot_use("bias")
                P.op("tensor", lambda e, j=j: e.matmul(
                    ps_quad[:, j * 1024:j * 1024 + 1], onesr[0:1, :], kmax[0:1, 2:3],
                    start=True, stop=True),
                    waits=[(sa_v, t_ready)] + ([pf] if pf else []), inc=sa_t)
                bc_ready = sa_t.n
                P.wait("scalar", [(sa_t, bc_ready)])
                P.op("scalar", lambda e, b=b, j=j: e.activation(
                    out=cbias[b][:, :], in_=ps_quad[:, j * 1024:j * 1024 + 1],
                    func=AF.Sqrt, scale=0.01625625))
                P.op("scalar", lambda e, b=b: e.mul(cbias[b][:, :], cbias[b][:, :], -1.0),
                     inc=sa_s)
                state["pst_free"][j] = (sa_s, sa_s.n)
                bias_done[hp] = (sa_s, sa_s.n)
                yield

            def emit_B_quad(hp, qq, bias_rd, prev_epi):
                b = hp % 2
                nkt = 4 * qq + 4
                pt_ready = {}

                def emit_ST(kt):
                    j, pf = st_slot_use("st")
                    off = max(0, (kt - 4 * qq) * 128)
                    w0 = ([pf] if pf else [])
                    if qq == 0 and kt == 0:
                        w0 = w0 + [ld_done[hp]]
                    for i in range(2):
                        P.op("tensor", lambda e, b=b, kt=kt, qq=qq, j=j, i=i, off=off:
                             e.matmul(
                            ps_quad[:, j * 1024 + i * 512 + off:j * 1024 + (i + 1) * 512],
                            akp[b][i * 64:(i + 1) * 64, kt * 128:(kt + 1) * 128],
                            aqp[b][i * 64:(i + 1) * 64,
                                   qq * 512 + off:(qq + 1) * 512],
                            start=True, stop=True),
                            waits=(w0 if i == 0 else []),
                            inc=(sa_t if i == 1 else None))
                    return j

                def emit_exp(kt, j):
                    slot = (2 * kt) % 6
                    st_ready = sa_t.n
                    diag = (kt >= 4 * qq)
                    P.wait("scalar", [(sa_t, st_ready), bias_rd,
                                      (sa_t, state["pt_free"].get(slot, 0)),
                                      (sa_t, state["pt_free"].get(slot + 1, 0))])
                    if not diag:
                        P.op("scalar", lambda e, slot=slot, j=j, b=b: e.activation(
                            out=PT[:, slot * 512:(slot + 2) * 512],
                            in_=ps_quad[:, j * 1024:(j + 1) * 1024],
                            func=AF.Exp, scale=0.125, bias=cbias[b][:, 0:1]), inc=sa_s)
                        state["pst_free"][j] = (sa_s, sa_s.n)
                        pt_ready[2 * kt] = pt_ready[2 * kt + 1] = (sa_s, sa_s.n)
                    else:
                        off = (kt - 4 * qq) * 128
                        if off > 0:
                            P.op("scalar", lambda e, j=j, b=b, off=off: e.activation(
                                out=PTt[:, 0:1024].rearrange(
                                    "p (bk c) -> p bk c", c=512)[:, :, off:512],
                                in_=ps_quad[:, j * 1024:(j + 1) * 1024].rearrange(
                                    "p (bk c) -> p bk c", c=512)[:, :, off:512],
                                func=AF.Exp, scale=0.125, bias=cbias[b][:, 0:1]),
                                waits=[(sa_v, state["ptt_free"])], inc=sa_s)
                        else:
                            P.op("scalar", lambda e, j=j, b=b: e.activation(
                                out=PTt[:, 0:1024],
                                in_=ps_quad[:, j * 1024:(j + 1) * 1024],
                                func=AF.Exp, scale=0.125, bias=cbias[b][:, 0:1]),
                                waits=[(sa_v, state["ptt_free"])], inc=sa_s)
                        state["pst_free"][j] = (sa_s, sa_s.n)
                        vv = kt - 4 * qq
                        for i in range(2):
                            P.op("vector", lambda e, i=i, slot=slot, vv=vv: e.tensor_mul(
                                PT[:, (slot + i) * 512:(slot + i + 1) * 512],
                                PTt[:, i * 512:(i + 1) * 512],
                                c_bmask[:, 384 - 128 * vv:896 - 128 * vv]),
                                waits=[(sa_s, sa_s.n),
                                       (sa_t, state["pt_free"].get(slot + i, 0))],
                                inc=sa_v)
                            pt_ready[2 * kt + i] = (sa_v, sa_v.n)
                        state["ptt_free"] = sa_v.n

                def emit_PV(kt):
                    slot = (2 * kt) % 6
                    off = max(0, (kt - 4 * qq) * 128)
                    for i in range(2):
                        po = ps_o if i == 0 else ps_m
                        w = [pt_ready[2 * kt + i]]
                        if kt == 0:
                            w.append((sa_v, state["pso_free_v"]))
                        P.op("tensor", lambda e, b=b, kt=kt, slot=slot, i=i, po=po,
                             off=off, first=(kt == 0), last=(kt == nkt - 1): e.matmul(
                            po[0:65, off:512],
                            avp[b][:, kt * 130 + i * 65:kt * 130 + i * 65 + 65],
                            PT[:, (slot + i) * 512 + off:(slot + i + 1) * 512],
                            start=first, stop=last),
                            waits=w, inc=(sa_t if i == 1 else None))
                    state["pt_free"][slot] = sa_t.n
                    state["pt_free"][slot + 1] = sa_t.n

                jmap = {}
                for kt in range(nkt):
                    jmap[kt] = emit_ST(kt)
                    emit_exp(kt, jmap[kt])
                    if kt == 0 and prev_epi is not None:
                        prev_epi()
                    if kt >= 2:
                        emit_PV(kt - 2)
                    pump(2)
                emit_PV(nkt - 2)
                emit_PV(nkt - 1)
                pv_done = sa_t.n

                def epilogue():
                    P.wait("vector", [(sa_t, pv_done)])
                    P.op("vector", lambda e: e.tensor_copy(otsb[0:65, 0:512],
                                                           ps_o[0:65, :]))
                    P.op("vector", lambda e: e.tensor_copy(otsb[0:65, 512:1024],
                                                           ps_m[0:65, :]), inc=sa_v)
                    state["pso_free_v"] = sa_v.n
                    o_ready = sa_v.n
                    P.wait("tensor", [(sa_v, o_ready), (sa_v, state["pta_free"])])
                    for sub in range(8):
                        P.op("tensor", lambda e, sub=sub: e.transpose(
                            pt_a[:, sub * 66:sub * 66 + 65],
                            otsb[0:65, sub * 128:(sub + 1) * 128],
                            c_ident[0:65, 0:65]),
                            inc=(sa_t if sub == 7 else None))
                    tr_done = sa_t.n
                    P.wait("vector", [(sa_t, tr_done)])
                    P.op("vector", lambda e: e.reciprocal(
                        invl[:, 0:8], pt_a[:, 0:528].rearrange(
                            "p (s c) -> p s c", c=66)[:, :, 64:65]), inc=sa_v)
                    P.wait("vector", [(sa_v, sa_v.n)])
                    for sub in range(8):
                        i = sub // 4
                        ti = qq * 4 + (sub % 4)
                        h = 2 * hp + i
                        P.op("vector", lambda e, sub=sub, ti=ti, h=h: e.tensor_scalar(
                            O_all[:, ti * D + h * 64:ti * D + h * 64 + 64],
                            pt_a[:, sub * 66:sub * 66 + 64],
                            invl[:, sub:sub + 1], None,
                            ALU.mult), inc=(sa_v if sub == 7 else None))
                    state["pta_free"] = sa_v.n

                return epilogue

            gen = bound_prep_units(0)
            for _ in gen:
                pass
            GEN_BOX = [None]

            def pump(k):
                g = GEN_BOX[0]
                if g is None:
                    return
                for _ in range(k):
                    try:
                        next(g)
                    except StopIteration:
                        GEN_BOX[0] = None
                        return

            epi = None
            for hp in range(NP):
                GEN_BOX[0] = bound_prep_units(hp + 1) if hp + 1 < NP else None
                for qq in range(NQ):
                    epi = emit_B_quad(hp, qq, bias_done[hp], epi)
                while GEN_BOX[0] is not None:
                    pump(1)
                pair_end_t[hp] = sa_t.n
                pair_end_v[hp] = sa_v.n
                pair_end_s[hp] = sa_s.n
            epi()
            if oall_tap is not None:
                P.op("sync", lambda e: e.dma_start(out=oall_tap[:, :], in_=O_all[:, :]),
                     waits=[(sa_v, sa_v.n)], inc=s_tap, inck=16)
            sta.close()
            barrier("b_attn", [(s_tap, s_tap.n)])

        sw_w = sem("sw_w")
        sw_ld = sem2("sw_ld")
        sw_t = sem("sw_t")
        sw_v = sem("sw_v")
        sw_s = sem("sw_s")
        sw_st = sem2("sw_st")
        X1_ST_ALL = []
        if "wo" in phases:
            stw = ExitStack()
            wosb = sb(stw, "wosb", [128, 8 * 1024], BF16)
            ot2 = [sb(stw, "ot2_%d" % i, [128, 1024], BF16) for i in range(2)]
            otst = [sb(stw, "otst%d" % i, [128, 1024], BF16) for i in range(2)]
            xin2 = [sb(stw, "w_xin%d" % i, [128, D], F32) for i in range(2)]
            x1sb = [sb(stw, "w_x1%d" % i, [128, D], F32) for i in range(2)]
            ln2 = make_ln_bufs(stw, "w_")
            wo3 = wo_h.rearrange("(ft p) c -> p ft c", p=128)
            P.op("gpsimd", lambda e: e.dma_start(
                out=wosb[:, :].rearrange("p (ft c) -> p ft c", c=1024), in_=wo3[:, :, :]),
                inc=sw_w, inck=16)
            sems2 = {"v": sw_v, "s": sw_s, "t": sw_t, "tr_done": {}, "tp_done": {}}
            mm_free = {}
            v_done2 = {}
            st_done2 = {}
            otr_t = {}
            for ti in range(NT):
                b = ti % 2
                ptb = pt_a if ti % 2 == 0 else pt_b
                P.op("sync", lambda e, ti=ti, b=b: e.dma_start(
                    out=xin2[b][:, :], in_=xpu_h[ti * 128:(ti + 1) * 128, :]),
                    waits=list(XPU_ST_ALL) + [(sw_v, v_done2.get(ti - 2, 0))],
                    inc=sw_ld[b], inck=16)
                xin_ready = (sw_ld[b], sw_ld[b].n)
                P.op("vector", lambda e, ti=ti, b=b: e.tensor_copy(
                    otst[b][:, :], O_all[:, ti * D:(ti + 1) * D]),
                    waits=[(sw_t, otr_t.get(ti - 2, 0))], inc=sw_v)
                otst_ready = sw_v.n
                trd2 = sems2["tr_done"].get(ti - 2)
                P.wait("tensor", ([trd2] if trd2 else []) + [(sw_v, otst_ready)])
                for f in range(8):
                    P.op("tensor", lambda e, b=b, f=f, ptb=ptb: e.transpose(
                        ptb[:, f * 128:(f + 1) * 128],
                        otst[b][:, f * 128:(f + 1) * 128], c_ident[:, :]),
                        inc=(sw_t if f == 7 else None))
                otr_t[ti] = sw_t.n
                P.op("vector", lambda e, b=b, ptb=ptb: e.tensor_copy(ot2[b][:, :], ptb[:, :]),
                     waits=[(sw_t, sw_t.n)], inc=sw_v)
                ot2_ready = sw_v.n
                P.wait("tensor", [(sw_v, ot2_ready), (sw_w, 16),
                                  (sw_v, mm_free.get(ti - 2, 0))])
                pss = ((ps_a, ps_b) if ti % 2 == 0
                       else (ps_st[:, 0:512], ps_st[:, 512:1024]))
                for oh in range(2):
                    for f in range(8):
                        P.op("tensor", lambda e, b=b, oh=oh, f=f, pss=pss: e.matmul(
                            _ap(pss[oh]), ot2[b][:, f * 128:(f + 1) * 128],
                            wosb[:, f * 1024 + oh * 512:f * 1024 + oh * 512 + 512],
                            start=(f == 0), stop=(f == 7)),
                            inc=(sw_t if (oh == 1 and f == 7) else None))
                wo_done = sw_t.n
                stdp = st_done2.get(ti - 2)
                P.wait("vector", [(sw_t, wo_done), xin_ready] + ([stdp] if stdp else []))
                for oh in range(2):
                    P.op("vector", lambda e, b=b, oh=oh, pss=pss: e.tensor_add(
                        x1sb[b][:, oh * 512:(oh + 1) * 512], _ap(pss[oh]),
                        xin2[b][:, oh * 512:(oh + 1) * 512]),
                        inc=(sw_v if oh == 1 else None))
                mm_free[ti] = sw_v.n
                x1_ready = sw_v.n
                P.op("sync", lambda e, ti=ti, b=b: e.dma_start(
                    out=x1_h[ti * 128:(ti + 1) * 128, :], in_=x1sb[b][:, :]),
                    waits=[(sw_v, x1_ready)], inc=sw_st[b], inck=16)
                st_done2[ti] = (sw_st[b], sw_st[b].n)
                emit_ln_tile(ti, ln2, x1sb[b][:, :], arenaA, False, sems2)
                v_done2[ti] = sw_v.n
            X1_ST_ALL = [(c, c.n) for c in sw_st]
            if u2T_tap is not None:
                P.op("sync", lambda e: e.dma_start(out=u2T_tap[:, :], in_=arenaA[:, :]),
                     waits=[(sw_s, sw_s.n)], inc=s_tap, inck=16)
            u2_chunk = {n: sems2["tr_done"][min(4 * n + 3, NT - 1)] for n in range(8)}
            WO_PS_FREE = (sw_v, sw_v.n)
        else:
            u2_chunk = {}
            WO_PS_FREE = (None, 0)

        sfa_w = sem2("sfa_w")
        sfa_t = sem("sfa_t")
        sfa_s = sem("sfa_s")
        sfa_v = sem("sfa_v")
        sfa_st = sem2("sfa_st")
        H0_ST_ALL = []
        if "fa" in phases:
            stf = ExitStack()
            wA = [sb(stf, "faw%d" % i, [128, 2048], BF16) for i in range(2)]
            silsb = [sb(stf, "fasil%d" % i, [128, 512], BF16) for i in range(2)]
            h0row = [sb(stf, "fah0", [128, 4096], BF16)] * 2
            w1r = w1t_h.rearrange("(dt p) c -> p dt c", p=128)
            w2r = w2t_h.rearrange("(dt p) c -> p dt c", p=128)
            mm_done = {}
            st_done = {}
            sil_done = {}
            ps_free = {}
            kl = 0
            for j in range(32):
                b = j % 2
                P.op("gpsimd", lambda e, j=j, b=b: e.dma_start(
                    out=wA[b][:, 0:1024].rearrange("p (dt c) -> p dt c", c=128),
                    in_=w1r[:, :, j * 128:(j + 1) * 128]),
                    waits=[(sfa_t, mm_done.get(j - 2, 0))], inc=sfa_w[b], inck=16)
                P.op("gpsimd", lambda e, j=j, b=b: e.dma_start(
                    out=wA[b][:, 1024:2048].rearrange("p (dt c) -> p dt c", c=128),
                    in_=w2r[:, :, j * 128:(j + 1) * 128]), inc=sfa_w[b], inck=16)
                w_ready = (sfa_w[b], sfa_w[b].n)
                for n in range(8):
                    pa, pb = ((ps_a, ps_b) if n % 2 == 0
                              else (ps_st[:, 0:512], ps_st[:, 512:1024]))
                    uc = u2_chunk.get(n)
                    P.wait("tensor", [w_ready,
                                      (sfa_s, ps_free.get((kl - 2, "s"), 0)),
                                      (sfa_v, ps_free.get((kl - 2, "v"), 0))] +
                           ([uc] if uc else []) +
                           ([WO_PS_FREE] if kl < 2 and WO_PS_FREE[0] else []))
                    for dt in range(8):
                        P.op("tensor", lambda e, b=b, dt=dt, n=n, pa=pa: e.matmul(
                            _ap(pa), wA[b][:, dt * 128:(dt + 1) * 128],
                            arenaA[:, dt * S + n * 512:dt * S + n * 512 + 512],
                            start=(dt == 0), stop=(dt == 7)))
                    for dt in range(8):
                        P.op("tensor", lambda e, b=b, dt=dt, n=n, pb=pb: e.matmul(
                            _ap(pb), wA[b][:, 1024 + dt * 128:1024 + (dt + 1) * 128],
                            arenaA[:, dt * S + n * 512:dt * S + n * 512 + 512],
                            start=(dt == 0), stop=(dt == 7)),
                            inc=(sfa_t if dt == 7 else None))
                    mm_done[j] = sfa_t.n
                    P.op("scalar", lambda e, j=j, n=n, pa=pa: e.activation(
                        out=silsb[n % 2][:, :], in_=_ap(pa),
                        func=AF.Silu, scale=c_scl[:, j:j + 1]),
                        waits=[(sfa_t, sfa_t.n), (sfa_v, sil_done.get(kl - 2, 0))],
                        inc=sfa_s)
                    ps_free[(kl, "s")] = sfa_s.n
                    stdp = st_done.get(j - 1)
                    P.op("vector", lambda e, j=j, n=n, b=b, pb=pb: e.scalar_tensor_tensor(
                        h0row[b][:, n * 512:(n + 1) * 512], _ap(pb),
                        c_scl[:, 32 + j:33 + j], silsb[n % 2][:, :], ALU.mult, ALU.mult),
                        waits=[(sfa_s, sfa_s.n)] + ([stdp] if stdp else []),
                        inc=sfa_v)
                    ps_free[(kl, "v")] = sfa_v.n
                    sil_done[kl] = sfa_v.n
                    kl += 1
                P.op("sync", lambda e, j=j, b=b: e.dma_start(
                    out=h0_h[j * 128:(j + 1) * 128, :], in_=h0row[b][:, :]),
                    waits=[(sfa_v, sfa_v.n)], inc=sfa_st[b], inck=16)
                st_done[j] = (sfa_st[b], sfa_st[b].n)
            H0_ST_ALL = [(c, c.n) for c in sfa_st]
            stf.close()
        if "wo" in phases:
            stw.close()
        stB.close()
        stA.close()
        if "fa" in phases:
            barrier("b_fa", H0_ST_ALL + X1_ST_ALL + [(s_tap, s_tap.n)])
        elif "wo" in phases:
            barrier("b_wo", X1_ST_ALL + [(s_tap, s_tap.n)])

        def emit_hid_stage(tag, src_dram, wT_dram, dst_dram, scl_off, prev_st):
            s_ld = sem2(tag + "_ld")
            s_w = sem2(tag + "_w")
            s_t = sem(tag + "_t")
            s_s = sem(tag + "_s")
            s_st = sem2(tag + "_st")
            stg = ExitStack()
            hhalf = [sb(stg, tag + "hh%d" % i, [128, 16 * 2048], BF16) for i in range(2)]
            wB = [sb(stg, tag + "w%d" % i, [128, 32 * 128], BF16) for i in range(2)]
            harow = [sb(stg, tag + "r%d" % i, [128, 2048], BF16) for i in range(2)]
            srcr = src_dram.rearrange("(it p) t -> p it t", p=128)
            wr = wT_dram.rearrange("(it p) c -> p it c", p=128)
            half_mm = {}
            wj_mm = {}
            ps_free = {}
            row_st = {}
            kl = 0
            hh_ready = {}
            for H in range(2):
                hp = hhalf
                for hv in range(2):
                    P.op("sync", lambda e, H=H, hv=hv, hp=hp: e.dma_start(
                        out=hp[hv][:, :].rearrange("p (it t) -> p it t", t=2048),
                        in_=srcr[:, hv * 16:(hv + 1) * 16, H * 2048:(H + 1) * 2048]),
                        waits=(list(prev_st) + [(s_t, half_mm.get(H - 1, 0))]) if hv == 0
                        else [(s_t, half_mm.get(H - 1, 0))], inc=s_ld[hv], inck=16)
                    hh_ready[(H, hv)] = (s_ld[hv], s_ld[hv].n)
                for j in range(32):
                    jj = H * 32 + j
                    b = jj % 2
                    P.op("gpsimd", lambda e, j=j, b=b: e.dma_start(
                        out=wB[b][:, :].rearrange("p (it c) -> p it c", c=128),
                        in_=wr[:, :, j * 128:(j + 1) * 128]),
                        waits=[(s_t, wj_mm.get(jj - 2, 0))], inc=s_w[b], inck=16)
                    w_ready = (s_w[b], s_w[b].n)
                    for n in range(4):
                        ps = ps_a if kl % 2 == 0 else ps_b
                        hp = hhalf
                        P.wait("tensor", [hh_ready[(H, 0)], w_ready,
                                          (s_s, ps_free.get(kl - 2, 0))])
                        for it in range(16):
                            P.op("tensor", lambda e, b=b, it=it, n=n, ps=ps, hp=hp: e.matmul(
                                _ap(ps), wB[b][:, it * 128:(it + 1) * 128],
                                hp[0][:, it * 2048 + n * 512:it * 2048 + n * 512 + 512],
                                start=(it == 0), stop=False))
                        P.wait("tensor", [hh_ready[(H, 1)]])
                        for it in range(16, 32):
                            P.op("tensor", lambda e, b=b, it=it, n=n, ps=ps, hp=hp: e.matmul(
                                _ap(ps), wB[b][:, it * 128:(it + 1) * 128],
                                hp[1][:, (it - 16) * 2048 + n * 512:
                                       (it - 16) * 2048 + n * 512 + 512],
                                start=False, stop=(it == 31)),
                                inc=(s_t if it == 31 else None))
                        wj_mm[jj] = s_t.n
                        rsp = row_st.get(jj - 2)
                        P.op("scalar", lambda e, j=j, n=n, b=b, ps=ps: e.activation(
                            out=harow[b][:, n * 512:(n + 1) * 512], in_=_ap(ps),
                            func=AF.Gelu, scale=c_scl[:, scl_off + j:scl_off + j + 1]),
                            waits=[(s_t, s_t.n)] + ([rsp] if rsp else []),
                            inc=s_s)
                        ps_free[kl] = s_s.n
                        kl += 1
                    P.op("sync", lambda e, j=j, b=b, H=H: e.dma_start(
                        out=dst_dram[j * 128:(j + 1) * 128, H * 2048:(H + 1) * 2048],
                        in_=harow[b][:, :]),
                        waits=[(s_s, s_s.n)], inc=s_st[b], inck=16)
                    row_st[jj] = (s_st[b], s_st[b].n)
                half_mm[H] = s_t.n
            stg.close()
            return [(c, c.n) for c in s_st]

        fb_st = H0_ST_ALL
        if "fb" in phases:
            fb_st = emit_hid_stage("fb", h0_h, h1t_h, ha_h, 64, H0_ST_ALL)
            barrier("b_fb", fb_st)
        if "fc" in phases:
            fc_st = emit_hid_stage("fc", ha_h, h2t_h, hb_h, 96, fb_st)
            barrier("b_fc", fc_st)
        else:
            fc_st = fb_st

        sfd_w = sem("sfd_w")
        sfd_ld = sem2("sfd_ld")
        sfd_t = sem("sfd_t")
        sfd_v = sem("sfd_v")
        sfd_st = sem2("sfd_st")
        if "fd" in phases:
            std = ExitStack()
            c_s3b = sb(std, "c_s3b", [128, D], F32)
            s_s3b = sem("s_s3b")
            P.op("sync", lambda e: e.dma_start(out=c_s3b[:, :], in_=s3b_h[:, :]),
                 inc=s_s3b, inck=16)
            P.wait("vector", [(s_s3b, 16)])
            w3sb = sb(std, "fdw3", [128, 32 * 1024], BF16)
            hbc = [sb(std, "fdhb%d" % i, [128, 32 * 512], BF16) for i in range(2)]
            x1c = [sb(std, "fdx1%d" % i, [128, 4 * 1024], F32) for i in range(2)]
            outst = [sb(std, "fdo%d" % i, [128, D], F32) for i in range(2)]
            tmpd = [sb(std, "fdt%d" % i, [128, 512], F32) for i in range(2)]
            w3r = w3t_h.rearrange("(it p) c -> p it c", p=128)
            hbr = hb_h.rearrange("(it p) t -> p it t", p=128)
            x1r = x1_h.rearrange("(tc tt p) c -> tc p tt c", p=128, tt=4)
            P.op("gpsimd", lambda e: e.dma_start(
                out=w3sb[:, :].rearrange("p (it c) -> p it c", c=1024), in_=w3r[:, :, :]),
                inc=sfd_w, inck=16)
            tc_mm = {}
            tc_v = {}
            ps_free = {}
            tile_st = {}
            for tc in range(8):
                b = tc % 2
                P.op("sync", lambda e, tc=tc, b=b: e.dma_start(
                    out=hbc[b][:, :].rearrange("p (it t) -> p it t", t=512),
                    in_=hbr[:, :, tc * 512:(tc + 1) * 512]),
                    waits=list(fc_st) + [(sfd_t, tc_mm.get(tc - 2, 0))],
                    inc=sfd_ld[b], inck=16)
                P.op("sync", lambda e, tc=tc, b=b: e.dma_start(
                    out=x1c[b][:, :].rearrange("p (tt c) -> p tt c", c=1024),
                    in_=x1r[tc, :, :, :]),
                    waits=list(X1_ST_ALL) + [(sfd_v, tc_v.get(tc - 2, 0))],
                    inc=sfd_ld[b], inck=16)
                ld_ready = (sfd_ld[b], sfd_ld[b].n)
                for tt in range(4):
                    tile = tc * 4 + tt
                    b2 = tile % 2
                    for oh in range(2):
                        kp = tile * 2 + oh
                        ps = PS4[kp % 4]
                        P.wait("tensor", [ld_ready, (sfd_w, 16),
                                          (sfd_v, ps_free.get(kp - 4, 0))])
                        for it in range(32):
                            P.op("tensor", lambda e, b=b, it=it, tt=tt, oh=oh, ps=ps: e.matmul(
                                _ap(ps),
                                hbc[b][:, it * 512 + tt * 128:it * 512 + tt * 128 + 128],
                                w3sb[:, it * 1024 + oh * 512:it * 1024 + oh * 512 + 512],
                                start=(it == 0), stop=(it == 31)),
                                inc=(sfd_t if it == 31 else None))
                        tc_mm[tc] = sfd_t.n
                        tsp = tile_st.get(tile - 2)
                        P.wait("vector", [(sfd_t, sfd_t.n)] + ([tsp] if tsp else []))
                        P.op("vector", lambda e, oh=oh, ps=ps, b2=b2: e.tensor_mul(
                            tmpd[b2][:, :], _ap(ps), c_s3b[:, oh * 512:(oh + 1) * 512]))
                        P.op("vector", lambda e, oh=oh, b=b, b2=b2, tt=tt: e.tensor_add(
                            outst[b2][:, oh * 512:(oh + 1) * 512], tmpd[b2][:, :],
                            x1c[b][:, tt * 1024 + oh * 512:tt * 1024 + oh * 512 + 512]),
                            inc=sfd_v)
                        ps_free[kp] = sfd_v.n
                    tc_v[tc] = sfd_v.n
                    P.op("sync", lambda e, tile=tile, b2=b2: e.dma_start(
                        out=out_h[tile * 128:(tile + 1) * 128, :], in_=outst[b2][:, :]),
                        waits=[(sfd_v, sfd_v.n)], inc=sfd_st[b2], inck=16)
                    tile_st[tile] = (sfd_st[b2], sfd_st[b2].n)
            std.close()

        fw = [(c, c.n) for c in ([s_tap] + s0_st + sq_st + sq_st2 + sw_st + sfa_st + sfd_st)]
        P.wait("sync", fw)

        with nc.Block() as block:
            for ename in ENGS:
                fns = P.q[ename]
                getattr(block, ename)(lambda e, fns=fns: [f(e) for f in fns])

    return nc


_NC_CACHE = {}


def _get_nc(upto="fd", taps=()):
    key = (upto, tuple(taps))
    if key not in _NC_CACHE:
        _NC_CACHE[key] = build_nc(upto, taps)
    return _NC_CACHE[key]


def make_in_maps(inputs, n_cores=8):
    f32 = np.float32
    bf16 = ml_dtypes.bfloat16
    wq = np.asarray(inputs["L0_attn_wq"], f32).transpose(1, 0, 2).reshape(D, D)
    wk = np.asarray(inputs["L0_attn_wk"], f32).transpose(1, 0, 2).reshape(D, D)
    wv = np.asarray(inputs["L0_attn_wv"], f32).transpose(1, 0, 2).reshape(D, D)
    wqkv = np.ascontiguousarray(np.stack([wq, wk, wv]))
    wo = np.ascontiguousarray(np.asarray(inputs["L0_attn_wo"], f32))
    w1t = np.ascontiguousarray(np.asarray(inputs["L0_ffn_w1_w"], f32).T)
    w2t = np.ascontiguousarray(np.asarray(inputs["L0_ffn_w2_w"], f32).T)
    h1t = np.ascontiguousarray(np.asarray(inputs["L0_ffn_h1_w"], f32).T)
    h2t = np.ascontiguousarray(np.asarray(inputs["L0_ffn_h2_w"], f32).T)
    w3t = np.ascontiguousarray(np.asarray(inputs["L0_ffn_w3_w"], f32).T)
    lnw = np.asarray(inputs["L0_attn_ln_w"], f32).reshape(8, 128).T
    lnb = np.asarray(inputs["L0_attn_ln_b"], f32).reshape(8, 128).T
    lnwb = np.ascontiguousarray(np.concatenate([lnw, lnb], axis=1))
    scl = np.ascontiguousarray(np.concatenate(
        [np.asarray(inputs[k], f32).reshape(32, 128).T
         for k in ("L0_ffn_w1_s", "L0_ffn_w2_s", "L0_ffn_h1_s", "L0_ffn_h2_s")], axis=1))
    s3b = np.ascontiguousarray(np.broadcast_to(
        np.asarray(inputs["L0_ffn_w3_s"], f32), (128, D)))
    ident = np.eye(128, dtype=bf16)
    nident = (-np.eye(128, dtype=np.float32)).astype(bf16)
    kk = np.arange(128)[:, None]
    xx2 = np.arange(896)[None, :] - 384
    bmask = np.ascontiguousarray((kk <= xx2).astype(np.float32).astype(bf16))
    konst = np.zeros((128, 8), f32)
    konst[:, :6] = np.array([1e-5, -0.5, 0.125, -1.0, 1.0, 1.0 / 1024.0], f32)

    x = np.asarray(inputs["L0_attn_x"], f32)
    shared = dict(wqkv=wqkv, wo=wo, w1t=w1t, w2t=w2t, h1t=h1t, h2t=h2t, w3t=w3t,
                  lnwb=lnwb, scl=scl, s3b=s3b, ident=ident, nident=nident, bmask=bmask,
                  konst=konst)
    in_maps = []
    for c in range(n_cores):
        m = dict(shared)
        m["x"] = np.ascontiguousarray(x[c])
        in_maps.append(m)
    return in_maps


def kernel(**inputs):
    nc = _get_nc("fd", ())
    in_maps = make_in_maps(inputs)
    res = run_bass_kernel_spmd(nc, in_maps, core_ids=list(range(8)))
    return np.stack([np.asarray(r["out"]) for r in res.results])
